# version 02
# baseline (speedup 1.0000x reference)
import numpy as np
import ml_dtypes
from contextlib import ExitStack

import concourse.bass as bass
import concourse.mybir as mybir
from concourse.bass_utils import run_bass_kernel_spmd

F32 = mybir.dt.float32
BF16 = mybir.dt.bfloat16
I32 = mybir.dt.int32
AF = mybir.ActivationFunctionType
ALU = mybir.AluOpType
AX = mybir.AxisListType.X

PI = float(np.pi)
TWO_PI = float(2.0 * np.pi)
HALF_PI = float(np.pi / 2.0)
LN_EPS = 1e-05
D = 1024
CTX = 4096
H1 = 2048
IFF = 2816
NT = 32
TSEQ = 16
NCORE = 8
ENGS = ("sync", "scalar", "vector", "tensor", "gpsimd")

DEBUG_TAPS = []
RACE_DETECT = False
P0_LIMIT = 99
DEBUG_SPILLS = False
SKIP_CASTS = False
SKIP_SET = set()
FULL_SERIAL = False


def _ts(i, w=128):
    return slice(i * w, (i + 1) * w)


class Em:

    def __init__(self, nc):
        self.nc = nc
        self.q = {e: [] for e in ENGS}
        self.sems = {}
        self.cnt = {}
        self.hw = {}
        self.last_on = {}
        self.g = None

    def mk(self, name):
        if name not in self.sems:
            self.sems[name] = self.g.enter_context(self.nc.semaphore(name))
            self.cnt[name] = 0
        return name

    def wait(self, eng, ev):
        if ev is None:
            return
        s, v, peng, is_async = ev
        if v <= 0 or (peng == eng == "tensor" and not is_async):
            return
        key = (eng, s)
        if self.hw.get(key, 0) >= v:
            return
        self.hw[key] = v
        h = self.sems[s]
        self.q[eng].append(lambda e, h=h, v=v: e.wait_ge(h, v))

    def op(self, eng, fn, deps=(), sem=None, inc=0, async_ev=False):
        if FULL_SERIAL:
            deps = list(deps) + ([self.last_ev] if getattr(self, "last_ev", None) else [])
            inc = max(inc, 1)
        if eng in ("vector", "scalar", "gpsimd") and sem is None:
            inc = max(inc, 1)
            prev = self.last_on.get(eng)
            if prev is not None:
                deps = list(deps) + [prev]
        for d in deps:
            self.wait(eng, d)
        if inc:
            s = self.mk(sem if sem is not None else ("s_" + eng))
            self.cnt[s] = self.cnt.get(s, 0) + inc
            v = self.cnt[s]
            h = self.sems[s]
            self.q[eng].append(lambda e, fn=fn, h=h, inc=inc: fn(e).then_inc(h, inc))
            ev = (s, v, eng, True if FULL_SERIAL else async_ev)
            if FULL_SERIAL:
                self.last_ev = ev
            if eng in ("vector", "scalar", "gpsimd"):
                self.last_on[eng] = (s, v, eng, True)
            return ev
        self.q[eng].append(lambda e, fn=fn: fn(e))
        return None

    def dma(self, eng, out, in_, deps=(), slot=None):
        s = self.mk("d_" + (slot or eng))
        if FULL_SERIAL:
            deps = list(deps) + ([self.last_ev] if getattr(self, "last_ev", None) else [])
        for d in deps:
            self.wait(eng, d)
        self.cnt[s] = self.cnt.get(s, 0) + 16
        v = self.cnt[s]
        h = self.sems[s]
        self.q[eng].append(
            lambda e, out=out, in_=in_, h=h: e.dma_start(out=out, in_=in_).then_inc(h, 16))
        ev = (s, v, eng, True)
        if FULL_SERIAL:
            self.last_ev = ev
        return ev

    def dmat(self, eng, out, in_, deps=(), slot=None):
        s = self.mk("d_" + (slot or eng))
        if FULL_SERIAL:
            deps = list(deps) + ([self.last_ev] if getattr(self, "last_ev", None) else [])
        for d in deps:
            self.wait(eng, d)
        self.cnt[s] = self.cnt.get(s, 0) + 16
        v = self.cnt[s]
        h = self.sems[s]
        self.q[eng].append(
            lambda e, out=out, in_=in_, h=h: e.dma_start_transpose(out, in_).then_inc(h, 16))
        ev = (s, v, eng, True)
        if FULL_SERIAL:
            self.last_ev = ev
        return ev

    def barrier(self):
        for eng in ENGS:
            for s, v in list(self.cnt.items()):
                self.wait(eng, (s, v, "__b__", True))

    def flush(self, block):
        for eng in ENGS:
            ops = self.q[eng]
            if not ops:
                continue
            self.q[eng] = []
            getattr(block, eng)(lambda e, ops=ops: [f(e) for f in ops])


def build_program(taps=(), stop_after=None):
    nc = bass.Bass("TRN2", target_bir_lowering=False, debug=False,
                   detect_race_conditions=RACE_DETECT)
    g = ExitStack()
    S = g.enter_context

    d_x = nc.dram_tensor("x", [NT * 128, D], F32, kind="ExternalInput").ap()
    d_Wom = nc.dram_tensor("W_omega", [D, D], F32, kind="ExternalInput").ap()
    d_bom = nc.dram_tensor("b_omega", [1, D], F32, kind="ExternalInput").ap()
    d_ls = nc.dram_tensor("log_scale", [1, D], F32, kind="ExternalInput").ap()
    d_gam = nc.dram_tensor("ln_gamma", [1, CTX], F32, kind="ExternalInput").ap()
    d_beta = nc.dram_tensor("ln_beta", [32, 128], F32, kind="ExternalInput").ap()
    d_W1 = nc.dram_tensor("W1", [H1, CTX], F32, kind="ExternalInput").ap()
    d_b1 = nc.dram_tensor("b1", [1, H1], F32, kind="ExternalInput").ap()
    d_W2 = nc.dram_tensor("W2", [D, H1], F32, kind="ExternalInput").ap()
    d_b2 = nc.dram_tensor("b2", [1, D], F32, kind="ExternalInput").ap()
    d_cx = nc.dram_tensor("clip_x", [1, 1], F32, kind="ExternalInput").ap()
    d_cd = nc.dram_tensor("clip_down", [1, 1], F32, kind="ExternalInput").ap()
    d_wg = nc.dram_tensor("w_gate", [IFF, D], F32, kind="ExternalInput").ap()
    d_wu = nc.dram_tensor("w_up", [IFF, D], F32, kind="ExternalInput").ap()
    d_wd = nc.dram_tensor("w_down", [D, IFF], F32, kind="ExternalInput").ap()
    d_idb = nc.dram_tensor("idb", [128, 128], BF16, kind="ExternalInput").ap()
    d_idf = nc.dram_tensor("idf", [128, 128], F32, kind="ExternalInput").ap()
    d_ltri = nc.dram_tensor("ltri", [128, 128], BF16, kind="ExternalInput").ap()
    d_onesb = nc.dram_tensor("onesb", [1, 128], BF16, kind="ExternalInput").ap()
    d_onesf = nc.dram_tensor("onesf", [1, 128], F32, kind="ExternalInput").ap()
    d_cmw = nc.dram_tensor("cmw", [1, TSEQ * 128], BF16, kind="ExternalInput").ap()
    d_posc = nc.dram_tensor("posc", [TSEQ, 128], F32, kind="ExternalInput").ap()
    d_invp = nc.dram_tensor("invp", [TSEQ, 128], F32, kind="ExternalInput").ap()
    d_cons = nc.dram_tensor("cons", [128, 4], F32, kind="ExternalInput").ap()

    d_y = nc.dram_tensor("y", [NT * 128, D], F32, kind="ExternalOutput").ap()
    spk = dict(kind="ExternalOutput") if DEBUG_SPILLS else {}
    d_wombf = nc.dram_tensor("wombf", [D, D], BF16).ap()
    d_w1bf = nc.dram_tensor("w1bf", [H1, CTX], BF16).ap()
    d_w2bf = nc.dram_tensor("w2bf", [D, H1], BF16).ap()
    d_qwg = nc.dram_tensor("qwg", [IFF, D], BF16, **spk).ap()
    d_qwu = nc.dram_tensor("qwu", [IFF, D], BF16).ap()
    d_qwd = nc.dram_tensor("qwd", [D, IFF], BF16).ap()
    d_ctxc = nc.dram_tensor("ctxc", [NT * 128, CTX], BF16, **spk).ap()
    d_gla = nc.dram_tensor("gla", [NT * 128, H1 // 2], BF16, **spk).ap()
    d_glb = nc.dram_tensor("glb", [NT * 128, H1 // 2], BF16).ap()
    d_xmid = nc.dram_tensor("xmid", [NT * 128, D], F32, **spk).ap()
    d_t = nc.dram_tensor("tbuf", [NT * 128, IFF], F32, **spk).ap()
    cc1i = nc.dram_tensor("cc1i", [1, 128], F32)
    cc1o = nc.dram_tensor("cc1o", [1, 128], F32)
    cc2i = nc.dram_tensor("cc2i", [1, 128], F32)
    cc2o = nc.dram_tensor("cc2o", [1, 128], F32)

    tap_d = {}
    tapdefs = {
        "x1": (D, F32), "sin": (D, F32), "cos": (D, F32), "phir": (D, F32),
        "memr": (D, F32), "memi": (D, F32), "ctx": (CTX, BF16),
        "ctxc": (CTX, BF16), "h1": (H1 // 2, F32), "gl": (H1 // 2, BF16),
        "xmid": (D, F32), "q": (D, BF16), "t": (IFF, F32), "q2": (IFF, BF16),
    }
    for name in taps:
        if name == "scal":
            tap_d["scal"] = nc.dram_tensor("tap_scal", [1, 16], F32,
                                           kind="ExternalOutput").ap()
        else:
            w, dt = tapdefs[name]
            tap_d[name] = nc.dram_tensor("tap_" + name, [NT * 128, w], dt,
                                         kind="ExternalOutput").ap()

    E = Em(nc)
    E.g = g
    for s in ("s_tensor", "s_vector", "s_scalar", "s_gpsimd", "s_cc"):
        E.sems[s] = S(nc.semaphore(s))
        E.cnt[s] = 0

    IDB = S(nc.sbuf_tensor("IDB", [128, 128], BF16))
    IDF = S(nc.sbuf_tensor("IDF", [128, 128], F32))
    LTRI = S(nc.sbuf_tensor("LTRI", [128, 128], BF16))
    ONESB = S(nc.sbuf_tensor("ONESB", [1, 128], BF16))
    ONESF = S(nc.sbuf_tensor("ONESF", [1, 128], F32))
    POSC = S(nc.sbuf_tensor("POSC", [128, TSEQ], F32))
    INVPC = S(nc.sbuf_tensor("INVPC", [128, TSEQ], F32))
    CONS = S(nc.sbuf_tensor("CONS", [128, 4], F32))
    MUX = S(nc.sbuf_tensor("MUX", [128, NT], F32))
    RSTDX = S(nc.sbuf_tensor("RSTDX", [128, NT], F32))
    VARC = S(nc.sbuf_tensor("VARC", [128, NT], F32))
    RSTDC = S(nc.sbuf_tensor("RSTDC", [128, NT], F32))
    MU2 = S(nc.sbuf_tensor("MU2", [128, NT], F32))
    RSTD2 = S(nc.sbuf_tensor("RSTD2", [128, NT], F32))
    RSTD2S = S(nc.sbuf_tensor("RSTD2S", [128, NT], F32))
    TMPA = S(nc.sbuf_tensor("TMPA", [128, NT], F32))
    TMPB = S(nc.sbuf_tensor("TMPB", [128, NT], F32))
    STT = S(nc.sbuf_tensor("STT", [128, 8], F32))
    AMX = S(nc.sbuf_tensor("AMX", [128, 2], F32))
    AMT = S(nc.sbuf_tensor("AMT", [128, 2], F32))
    SCAL = S(nc.sbuf_tensor("SCAL", [1, 16], F32))
    CCROW = S(nc.sbuf_tensor("CCROW", [1, 128], F32))
    SXCOL = S(nc.sbuf_tensor("SXCOL", [128, 1], F32))
    INVSX = S(nc.sbuf_tensor("INVSX", [128, 1], F32))
    SDCOL = S(nc.sbuf_tensor("SDCOL", [128, 1], F32))
    INVSD = S(nc.sbuf_tensor("INVSD", [128, 1], F32))
    XB = [S(nc.sbuf_tensor(f"XBUF{i}", [128, D], F32)) for i in range(2)]
    XM = [S(nc.sbuf_tensor(f"XMB{i}", [128, D], F32)) for i in range(2)]
    TB = [S(nc.sbuf_tensor(f"TBB{i}", [128, IFF], F32)) for i in range(2)]
    X1 = S(nc.sbuf_tensor("X1", [128, D], F32))
    SCR = S(nc.sbuf_tensor("SCRB", [128, D], F32))
    X2S = S(nc.sbuf_tensor("X2S", [128, D], F32))

    nc.const_aps.register(F32, HALF_PI, CONS[:, 0:1])
    nc.const_aps.register(F32, -1.0, CONS[:, 1:2])
    nc.const_aps.register(F32, -0.5, CONS[:, 2:3])

    x_free = [None, None]
    xm_free = [None, None]
    tb_free = [None, None]

    with g, nc.Block() as block:
        p1 = ExitStack()
        P = p1.enter_context
        BETA_TB = S(nc.sbuf_tensor("BETA_TB", [128, 32], BF16))
        BE = P(nc.sbuf_tensor("BE", [1, D], BF16))
        WTE = P(nc.sbuf_tensor("WTE", [128, 8 * D], BF16))
        GAMREP = P(nc.sbuf_tensor("GAMREP", [128, CTX], BF16))
        CMW = P(nc.sbuf_tensor("CMW", [1, TSEQ * 128], BF16))
        p0 = ExitStack()
        P0 = p0.enter_context
        POSL = P0(nc.sbuf_tensor("POSL", [TSEQ, 128], F32))
        INVL = P0(nc.sbuf_tensor("INVL", [TSEQ, 128], F32))
        BETA_L = P0(nc.sbuf_tensor("BETA_L", [32, 128], F32))
        ROWF = P0(nc.sbuf_tensor("ROWF", [1, 3 * D], F32))
        GROWB = P0(nc.sbuf_tensor("GROWB", [1, CTX], BF16))
        EREP = P0(nc.sbuf_tensor("EREP", [128, D], F32))
        WTES = P0(nc.sbuf_tensor("WTES", [128, 8 * D], BF16))
        p_t16 = P0(nc.psum_tensor("p_t16", [128, 64], F32))
        p_bc = P0(nc.psum_tensor("p_bc", [128, 1024], F32))

        e_ms = None
        if "act" not in SKIP_SET:
            e_ms = E.op("vector", lambda v: v.memset(SCAL[0:1, 0:16], 0.0), inc=1)
        if "cst" not in SKIP_SET:
            for dst, srca in ((IDB, d_idb), (IDF, d_idf), (LTRI, d_ltri),
                              (ONESB, d_onesb), (ONESF, d_onesf), (CMW, d_cmw),
                              (POSL, d_posc), (INVL, d_invp), (CONS, d_cons),
                              (BETA_L, d_beta)):
                E.dma("sync", dst[:], srca, slot="cst")
            E.dma("sync", ROWF[0:1, 0:D], d_bom, slot="cst")
            E.dma("sync", ROWF[0:1, 1024:2048], d_ls, slot="cst")
            E.dma("sync", SCAL[0:1, 8:9], d_cx, slot="cst", deps=[e_ms])
            E.dma("sync", SCAL[0:1, 9:10], d_cd, slot="cst", deps=[e_ms])
        else:
            E.mk("d_cst")
        ALLC = ("d_cst", E.cnt["d_cst"], "__x__", True)
        ev_clx = ALLC
        ev_cld = ALLC

        ev_wombf = ev_w2bf = ev_w1bf = None
        if P0_LIMIT >= 2 and not SKIP_CASTS:
            ev_wombf = E.dma("gpsimd", d_wombf, d_Wom, slot="cw0")
            ev_w2bf = E.dma("gpsimd", d_w2bf, d_W2, slot="cw1")
            ev_w1bf = E.dma("gpsimd", d_w1bf, d_W1, slot="cw2")

        T2 = TSEQ
        if "tp" in SKIP_SET:
            e_posc = e_invp = e_beta = None
        if P0_LIMIT < 3:
            E.barrier(); E.flush(block); p0.close(); p1.close(); return nc
        if "tp" not in SKIP_SET:
            ev1 = E.op("tensor", lambda t: t.transpose(p_t16[:, 0:T2], POSL[:, :],
                                                       IDF[0:T2, 0:T2]),
                       deps=[ALLC], inc=1)
            e_posc = E.op("vector", lambda v: v.tensor_copy(POSC[:], p_t16[:, 0:T2]),
                          deps=[ev1], inc=1)
            ev2 = E.op("tensor", lambda t: t.transpose(p_t16[:, T2:2 * T2], INVL[:, :],
                                                       IDF[0:T2, 0:T2]),
                       deps=[e_posc], inc=1)
            e_invp = E.op("vector", lambda v: v.tensor_copy(INVPC[:], p_t16[:, T2:2 * T2]),
                          deps=[ev2], inc=1)
            ev3 = E.op("tensor", lambda t: t.transpose(p_t16[:, 2 * T2:2 * T2 + 32],
                                                       BETA_L[:, :],
                                                       IDF[0:32, 0:32]),
                       deps=[e_invp], inc=1)
            e_beta = E.op("vector", lambda v: v.tensor_copy(BETA_TB[:],
                                                            p_t16[:, 2 * T2:2 * T2 + 32]),
                          deps=[ev3], inc=1)

        if P0_LIMIT < 4:
            E.barrier(); E.flush(block); p0.close(); p1.close(); return nc
        e_exp = e_sgx = e_sgd = e_bE = None
        if "act" not in SKIP_SET:
            e_exp = E.op("scalar", lambda s: s.activation(ROWF[0:1, 2048:3072],
                                                          ROWF[0:1, 1024:2048], AF.Exp),
                         deps=[ALLC], inc=1)
            e_sgx = E.op("scalar", lambda s: s.activation(SCAL[0:1, 10:11],
                                                          SCAL[0:1, 8:9], AF.Sigmoid),
                         deps=[ev_clx], inc=1)
            e_sgd = E.op("scalar", lambda s: s.activation(SCAL[0:1, 11:12],
                                                          SCAL[0:1, 9:10], AF.Sigmoid),
                         deps=[ev_cld], inc=1)
            e_bE = E.op("vector", lambda v: v.tensor_mul(BE[0:1, :], ROWF[0:1, 0:D],
                                                         ROWF[0:1, 2048:3072]),
                        deps=[e_exp], inc=1)
        if P0_LIMIT < 5:
            E.barrier(); E.flush(block); p0.close(); p1.close(); return nc
        gam_evs = []
        if "gam" not in SKIP_SET:
            e_gb = E.dma("gpsimd", GROWB[0:1, :], d_gam, slot="gml")
            for i in range(8):
                mm = E.op("tensor",
                          lambda t, i=i: t.matmul(p_bc[:, _ts(i % 2, 512)], ONESB[0:1, :],
                                                  GROWB[0:1, i * 512:(i + 1) * 512],
                                                  start=True, stop=True),
                          deps=[e_gb, gam_evs[-1]] if gam_evs else [e_gb], inc=1)
                gam_evs.append(E.op("vector",
                                    lambda v, i=i: v.tensor_copy(
                                        GAMREP[:, i * 512:(i + 1) * 512],
                                        p_bc[:, _ts(i % 2, 512)]),
                                    deps=[mm], inc=1))
        if P0_LIMIT < 6:
            E.barrier(); E.flush(block); p0.close(); p1.close(); return nc
        erep_evs = []
        for i in range(2):
            if "erep" in SKIP_SET or "act" in SKIP_SET:
                break
            mm = E.op("tensor",
                      lambda t, i=i: t.matmul(p_bc[:, _ts(i, 512)], ONESF[0:1, :],
                                              ROWF[0:1, 2048 + i * 512:2048 + (i + 1) * 512],
                                              start=True, stop=True),
                      deps=[e_exp] + gam_evs, inc=1)
            erep_evs.append(E.op("vector",
                                 lambda v, i=i: v.tensor_copy(EREP[:, _ts(i, 512)],
                                                              p_bc[:, _ts(i, 512)]),
                                 deps=[mm], inc=1))
        if P0_LIMIT < 7:
            E.barrier(); E.flush(block); p0.close(); p1.close(); return nc
        wte_evs = []
        for c in range(8):
            if SKIP_CASTS:
                continue
            E.dmat("sync", WTES[:, c * D:(c + 1) * D],
                   d_wombf[0:D, c * 128:(c + 1) * 128], deps=[ev_wombf],
                   slot="wte")
        wte_all = (("d_wte", E.cnt["d_wte"], "__x__", True)
                   if "d_wte" in E.cnt else None)
        for c in range(8):
            wte_evs.append(E.op("vector",
                                lambda v, c=c: v.tensor_mul(WTE[:, c * D:(c + 1) * D],
                                                            WTES[:, c * D:(c + 1) * D],
                                                            EREP[:]),
                                deps=[wte_all] + erep_evs, inc=1))

        if stop_after == "P0":
            E.barrier()
            E.flush(block)
            p0.close()
            p1.close()
            return nc

        prev_scr = None
        for j in range(NT):
            b = j % 2
            ld = E.dma("sync", XB[b][:], d_x[_ts(j), :],
                       deps=(x_free[b] if isinstance(x_free[b], list)
                             else [x_free[b]]) if x_free[b] else (), slot=f"x{b}")
            e1 = E.op("vector", lambda v, b=b: v.reduce_sum(STT[:, 0:1], XB[b][:], axis=AX),
                      deps=[ld], inc=1)
            e1b = E.op("vector", lambda v, j=j: v.tensor_scalar(
                MUX[:, j:j + 1], STT[:, 0:1], 1.0 / D, None, ALU.mult),
                deps=[e1], inc=1)
            e_sq = E.op("scalar", lambda s, b=b: s.activation(
                SCR[:], XB[b][:], AF.Square),
                deps=[ld] + ([prev_scr] if prev_scr else []), inc=1)
            e2 = E.op("vector", lambda v: v.reduce_sum(STT[:, 1:2], SCR[:], axis=AX),
                      deps=[e_sq], inc=1)
            prev_scr = e2
            x_free[b] = e2 if False else e_sq
            x_free[b] = [e1, e_sq]
            e3 = E.op("vector", lambda v, j=j: v.tensor_mul(
                STT[:, 2:3], MUX[:, j:j + 1], MUX[:, j:j + 1]), deps=[e2], inc=1)
            e4 = E.op("vector", lambda v: v.tensor_scalar(
                STT[:, 3:4], STT[:, 1:2], 1.0 / D, None, ALU.mult), deps=[e2], inc=1)
            e5 = E.op("vector", lambda v, j=j: v.tensor_sub(
                VARC[:, j:j + 1], STT[:, 3:4], STT[:, 2:3]), deps=[e3, e4], inc=1)

        def rstd_batch(VAR_IN, DST, dep):
            ev = E.op("vector", lambda v: v.tensor_scalar(
                TMPA[:, 0:NT], VAR_IN[:, 0:NT], LN_EPS, None, ALU.add),
                deps=[dep], inc=1)
            ev = E.op("scalar", lambda s: s.activation(DST[:, 0:NT], TMPA[:, 0:NT],
                                                       AF.Ln), deps=[ev], inc=1)
            ev = E.op("scalar", lambda s: s.activation(DST[:, 0:NT], DST[:, 0:NT],
                                                       AF.Exp, scale=-0.5),
                      deps=[ev], inc=1)
            ev = E.op("vector", lambda v: v.tensor_mul(TMPB[:, 0:NT], DST[:, 0:NT],
                                                       DST[:, 0:NT]), deps=[ev], inc=1)
            ev = E.op("vector", lambda v: v.tensor_mul(TMPB[:, 0:NT], TMPB[:, 0:NT],
                                                       TMPA[:, 0:NT]), deps=[ev], inc=1)
            ev = E.op("vector", lambda v: v.tensor_scalar(
                TMPB[:, 0:NT], TMPB[:, 0:NT], -0.5, 1.5, ALU.mult, ALU.add),
                deps=[ev], inc=1)
            return E.op("vector", lambda v: v.tensor_mul(DST[:, 0:NT], DST[:, 0:NT],
                                                         TMPB[:, 0:NT]), deps=[ev], inc=1)

        e_rstdx = e5 if "rstd" in SKIP_SET else rstd_batch(VARC, RSTDX, e5)

        if stop_after == "A0":
            E.barrier()
            if DEBUG_SPILLS:
                d_dbg3 = nc.dram_tensor("dbg_stats", [128, 4 * NT], F32,
                                        kind="ExternalOutput").ap()
                E.dma("sync", d_dbg3[:, 0:NT], MUX[:], slot="dbg")
                E.dma("sync", d_dbg3[:, NT:2 * NT], RSTDX[:], slot="dbg")
                E.dma("sync", d_dbg3[:, 2 * NT:3 * NT], TMPA[:], slot="dbg")
                E.dma("sync", d_dbg3[:, 3 * NT:4 * NT], TMPB[:], slot="dbg")
                d_dbg4 = nc.dram_tensor("dbg_xb", [256, D], F32,
                                        kind="ExternalOutput").ap()
                E.dma("sync", d_dbg4[0:128, :], XB[0][:], slot="dbg")
                E.dma("sync", d_dbg4[128:256, :], XB[1][:], slot="dbg")
                E.op("sync", lambda e: e.wait_ge(E.sems["d_dbg"], E.cnt["d_dbg"]))
            E.barrier()
            E.flush(block)
            p0.close()
            p1.close()
            return nc

        E.flush(block)
        p0.close()

        X1T = P(nc.sbuf_tensor("X1T", [128, D], BF16))
        OMS = P(nc.sbuf_tensor("OMS", [128, D], BF16))
        PHR = P(nc.sbuf_tensor("PHR", [128, D], F32))
        RRO = P(nc.sbuf_tensor("RRO", [128, D], F32))
        RRZ = P(nc.sbuf_tensor("RRZ", [128, D], F32))
        SIN = P(nc.sbuf_tensor("SIN", [128, D], F32))
        COSB = P(nc.sbuf_tensor("COSB", [128, D], F32))
        ABSB = P(nc.sbuf_tensor("ABSB", [128, D], F32))
        MEMR = P(nc.sbuf_tensor("MEMR", [128, D], F32))
        MEMI = P(nc.sbuf_tensor("MEMI", [128, D], F32))
        RT1 = P(nc.sbuf_tensor("RT1", [128, D], F32))
        RT2 = P(nc.sbuf_tensor("RT2", [128, D], F32))
        CTXB = P(nc.sbuf_tensor("CTXB", [128, CTX], BF16))
        CTXC = P(nc.sbuf_tensor("CTXC", [128, CTX], BF16))
        ROWB = P(nc.sbuf_tensor("ROWB", [1, 3 * D], BF16))
        p_om = P(nc.psum_tensor("p_om", [128, 1024], F32))
        p_phi = P(nc.psum_tensor("p_phi", [128, 1024], F32))
        p_cm = P(nc.psum_tensor("p_cm", [128, 2048], F32))

        pv = {"x1t": None, "oms": None, "phr": None, "memr": None, "memi": None,
              "rowphi": None, "rowmr": None, "rowmi": None, "sq": None,
              "ctxdma": None}
        tile_taps = []
        for j in range(NT):
            jq = j % TSEQ
            b = j % 2
            ld = E.dma("sync", XB[b][:], d_x[_ts(j), :],
                       deps=((x_free[b] if isinstance(x_free[b], list)
                              else [x_free[b]]) if x_free[b] else []) + tile_taps,
                       slot=f"x{b}")
            tile_taps = []
            e_x1 = E.op("vector", lambda v, b=b, j=j: v.tensor_scalar(
                X1[:], XB[b][:], MUX[:, j:j + 1], RSTDX[:, j:j + 1],
                ALU.subtract, ALU.mult), deps=[ld, e_rstdx], inc=1)
            x_free[b] = e_x1
            tev = None
            for c in range(8):
                tev = E.op("tensor", lambda t, c=c: t.transpose(
                    p_phi[:, _ts(c)], X1[:, _ts(c)], IDF[:]),
                    deps=[e_x1, pv["phr"]] if c == 0 else [],
                    inc=(1 if c == 7 else 0))
            e_x1t = E.op("vector", lambda v: v.tensor_copy(X1T[:], p_phi[:]),
                         deps=[tev], inc=1)
            pv["x1t"] = e_x1t
            first_deps = [e_x1t] + ([pv["oms"]] if pv["oms"] else []) \
                + (wte_evs if j == 0 else [])
            for n in range(2):
                for c in range(8):
                    E.op("tensor", lambda t, c=c, n=n: t.matmul(
                        p_om[:, _ts(n, 512)], X1T[:, _ts(c)],
                        WTE[:, c * D + n * 512: c * D + (n + 1) * 512],
                        start=(c == 0), stop=False),
                        deps=first_deps if (n == 0 and c == 0) else [])
            e_om = None
            for n in range(2):
                e_om = E.op("tensor", lambda t, n=n: t.matmul(
                    p_om[:, _ts(n, 512)], ONESB[0:1, :],
                    BE[0:1, n * 512:(n + 1) * 512],
                    start=False, stop=True),
                    deps=[e_bE] if (j == 0 and n == 0) else [],
                    inc=(1 if n == 1 else 0))
            e_oms = E.op("scalar", lambda s, jq=jq: s.activation(
                OMS[:], p_om[:], AF.Copy, scale=POSC[:, jq:jq + 1]),
                deps=[e_om, e_posc], inc=1)
            pv["oms"] = e_oms
            e_phi = None
            for n in range(2):
                e_phi = E.op("tensor", lambda t, n=n, jq=jq: t.matmul(
                    p_phi[:, _ts(n, 512)], LTRI[:], OMS[:, _ts(n, 512)],
                    start=True, stop=(jq == 0)),
                    deps=[e_oms, e_x1t] if n == 0 else [],
                    inc=(1 if (jq == 0 and n == 1) else 0))
            if jq > 0:
                for n in range(2):
                    e_phi = E.op("tensor", lambda t, n=n: t.matmul(
                        p_phi[:, _ts(n, 512)], ONESB[0:1, :],
                        ROWB[0:1, n * 512:(n + 1) * 512],
                        start=False, stop=True),
                        deps=[pv["rowphi"]] if n == 0 else [],
                        inc=(1 if n == 1 else 0))
            e_o1 = E.op("vector", lambda v: v.tensor_scalar(
                RRO[:], p_phi[:], PI, -TWO_PI, ALU.is_gt, ALU.mult),
                deps=[e_phi], inc=1)
            e_z1 = E.op("vector", lambda v: v.tensor_add(RRZ[:], p_phi[:], RRO[:]),
                        deps=[e_o1], inc=1)
            e_o2 = E.op("vector", lambda v: v.tensor_scalar(
                RRO[:], p_phi[:], -PI, TWO_PI, ALU.is_lt, ALU.mult),
                deps=[e_z1], inc=1)
            e_phr = E.op("vector", lambda v: v.tensor_add(PHR[:], RRZ[:], RRO[:]),
                         deps=[e_o2], inc=1)
            pv["phr"] = e_phr
            if jq != TSEQ - 1:
                pv["rowphi"] = E.dma("gpsimd", ROWB[0:1, 0:D], PHR[127:128, :],
                                     deps=[e_phr], slot="rp")
            e_sin = E.op("scalar", lambda s: s.activation(SIN[:], PHR[:], AF.Sin),
                         deps=[e_phr], inc=1)
            e_abs = E.op("scalar", lambda s: s.activation(ABSB[:], PHR[:], AF.Abs),
                         deps=[e_phr], inc=1)
            e_cos = E.op("scalar", lambda s: s.activation(
                COSB[:], ABSB[:], AF.Sin, bias=HALF_PI, scale=-1.0),
                deps=[e_abs], inc=1)
            e_cmr = E.op("vector", lambda v: v.tensor_mul(CTXB[:, 0:D], X1[:], COSB[:]),
                         deps=[e_cos] + ([pv["sq"]] if pv["sq"] else []), inc=1)
            e_cmi = E.op("vector", lambda v: v.tensor_mul(CTXB[:, D:2 * D], X1[:], SIN[:]),
                         deps=[e_sin], inc=1)
            cm_first = [e_cmr, e_cmi] + ([pv["memr"], pv["memi"]] if pv["memr"] else [])
            e_cm = None
            for (off, rowseg, rdep) in ((0, 1, "rowmr"), (1024, 2, "rowmi")):
                for n in range(2):
                    e_cm = E.op("tensor", lambda t, off=off, n=n, jq=jq: t.matmul(
                        p_cm[:, off + n * 512: off + (n + 1) * 512], LTRI[:],
                        CTXB[:, off + n * 512: off + (n + 1) * 512],
                        start=True, stop=(jq == 0)),
                        deps=cm_first if (off == 0 and n == 0) else [],
                        inc=(1 if (jq == 0 and off == 1024 and n == 1) else 0))
            if jq > 0:
                for (off, rowseg, rdep) in ((0, 1, "rowmr"), (1024, 2, "rowmi")):
                    for n in range(2):
                        e_cm = E.op("tensor", lambda t, off=off, rowseg=rowseg, n=n, jq=jq: t.matmul(
                            p_cm[:, off + n * 512: off + (n + 1) * 512],
                            CMW[0:1, jq * 128:(jq + 1) * 128],
                            ROWB[0:1, rowseg * 1024 + n * 512: rowseg * 1024 + (n + 1) * 512],
                            start=False, stop=True),
                            deps=[pv[rdep]] if n == 0 else [],
                            inc=(1 if (off == 1024 and n == 1) else 0))
            e_memr = E.op("scalar", lambda s, jq=jq: s.activation(
                MEMR[:], p_cm[:, 0:1024], AF.Copy, scale=INVPC[:, jq:jq + 1]),
                deps=[e_cm, e_invp], inc=1)
            e_memi = E.op("scalar", lambda s, jq=jq: s.activation(
                MEMI[:], p_cm[:, 1024:2048], AF.Copy, scale=INVPC[:, jq:jq + 1]),
                deps=[], inc=1)
            pv["memr"], pv["memi"] = e_memr, e_memi
            if jq != TSEQ - 1:
                pv["rowmr"] = E.dma("gpsimd", ROWB[0:1, 1024:2048], MEMR[127:128, :],
                                    deps=[e_memr], slot="rr")
                pv["rowmi"] = E.dma("gpsimd", ROWB[0:1, 2048:3072], MEMI[127:128, :],
                                    deps=[e_memi], slot="ri")
            e_r1 = E.op("vector", lambda v: v.tensor_mul(RT1[:], MEMR[:], COSB[:]),
                        deps=[e_memr], inc=1)
            e_r2 = E.op("vector", lambda v: v.tensor_mul(RT2[:], MEMI[:], SIN[:]),
                        deps=[e_memi], inc=1)
            e_rr = E.op("vector", lambda v: v.tensor_add(CTXB[:, 2 * D:3 * D],
                                                         RT1[:], RT2[:]),
                        deps=[e_r1, e_r2], inc=1)
            e_r3 = E.op("vector", lambda v: v.tensor_mul(RT1[:], MEMI[:], COSB[:]),
                        deps=[e_rr], inc=1)
            e_r4 = E.op("vector", lambda v: v.tensor_mul(RT2[:], MEMR[:], SIN[:]),
                        deps=[e_rr], inc=1)
            e_ri = E.op("vector", lambda v: v.tensor_sub(CTXB[:, 3 * D:4 * D],
                                                         RT1[:], RT2[:]),
                        deps=[e_r3, e_r4], inc=1)
            e_sq = E.op("scalar", lambda s: s.activation(
                CTXC[:], CTXB[:], AF.Square),
                deps=[e_ri] + ([pv["ctxdma"]] if pv["ctxdma"] else []), inc=1)
            e_ssq = E.op("vector", lambda v: v.reduce_sum(STT[:, 4:5], CTXC[:], axis=AX),
                         deps=[e_sq], inc=1)
            pv["sq"] = e_ssq
            e_sum = E.op("vector", lambda v: v.reduce_sum(STT[:, 5:6], CTXB[:], axis=AX),
                         deps=[e_ri], inc=1)
            e_mu = E.op("vector", lambda v: v.tensor_scalar(
                STT[:, 6:7], STT[:, 5:6], 1.0 / CTX, None, ALU.mult),
                deps=[e_sum], inc=1)
            e_m2 = E.op("vector", lambda v: v.tensor_mul(STT[:, 7:8], STT[:, 6:7],
                                                         STT[:, 6:7]),
                        deps=[e_mu], inc=1)
            e_vv = E.op("vector", lambda v, j=j: v.tensor_scalar(
                TMPA[:, j:j + 1], STT[:, 4:5], 1.0 / CTX, None, ALU.mult),
                deps=[e_ssq, e_m2], inc=1)
            e_var = E.op("vector", lambda v, j=j: v.tensor_sub(
                TMPA[:, j:j + 1], TMPA[:, j:j + 1], STT[:, 7:8]), deps=[e_vv], inc=1)
            e_ctxc = E.op("vector", lambda v: v.scalar_tensor_tensor(
                CTXC[:], CTXB[:], STT[:, 6:7], GAMREP[:], ALU.subtract, ALU.mult),
                deps=[e_ssq, e_mu] + (gam_evs if j == 0 else []), inc=1)
            pv["ctxdma"] = E.dma("gpsimd", d_ctxc[_ts(j), :], CTXC[:],
                                 deps=[e_ctxc], slot="sc")
            if "x1" in tap_d:
                tile_taps.append(E.dma("gpsimd", tap_d["x1"][_ts(j), :], X1[:], deps=[e_x1, e_cmi, e_cmr], slot="tap"))
            if "sin" in tap_d:
                tile_taps.append(E.dma("gpsimd", tap_d["sin"][_ts(j), :], SIN[:], deps=[e_sin, e_cmi, e_r2, e_r4], slot="tap"))
            if "cos" in tap_d:
                tile_taps.append(E.dma("gpsimd", tap_d["cos"][_ts(j), :], COSB[:], deps=[e_cos, e_cmr, e_r1, e_r3], slot="tap"))
            if "phir" in tap_d:
                tile_taps.append(E.dma("gpsimd", tap_d["phir"][_ts(j), :], PHR[:], deps=[e_phr, e_sin, e_abs], slot="tap"))
            if "memr" in tap_d:
                tile_taps.append(E.dma("gpsimd", tap_d["memr"][_ts(j), :], MEMR[:], deps=[e_r1, e_r4], slot="tap"))
            if "memi" in tap_d:
                tile_taps.append(E.dma("gpsimd", tap_d["memi"][_ts(j), :], MEMI[:], deps=[e_r2, e_r3], slot="tap"))
            if "ctx" in tap_d:
                tile_taps.append(E.dma("gpsimd", tap_d["ctx"][_ts(j), :], CTXB[:], deps=[e_ri, e_sq], slot="tap"))
            if "ctxc" in tap_d:
                tile_taps.append(E.dma("gpsimd", tap_d["ctxc"][_ts(j), :], CTXC[:], deps=[e_ctxc], slot="tap"))
        e_varc_last = e_var

        e_varc = E.op("vector", lambda v: v.tensor_copy(VARC[:, 0:NT], TMPA[:, 0:NT]),
                      deps=[e_varc_last, e_rstdx], inc=1)

        if stop_after == "A1" and DEBUG_SPILLS:
            E.barrier()
            d_dbg1 = nc.dram_tensor("dbg_gam", [128, CTX], BF16,
                                    kind="ExternalOutput").ap()
            d_dbg2 = nc.dram_tensor("dbg_wte", [128, 8 * D], BF16,
                                    kind="ExternalOutput").ap()
            d_dbg3 = nc.dram_tensor("dbg_stats", [128, 4 * NT], F32,
                                    kind="ExternalOutput").ap()
            E.dma("sync", d_dbg1, GAMREP[:], deps=[], slot="dbg")
            E.dma("sync", d_dbg2, WTE[:], deps=[], slot="dbg")
            E.dma("sync", d_dbg3[:, 0:NT], MUX[:], slot="dbg")
            E.dma("sync", d_dbg3[:, NT:2 * NT], RSTDX[:], slot="dbg")
            E.dma("sync", d_dbg3[:, 2 * NT:3 * NT], TMPA[:], slot="dbg")
            E.dma("sync", d_dbg3[:, 3 * NT:4 * NT], VARC[:], slot="dbg")
            E.op("sync", lambda e: e.wait_ge(E.sems["d_dbg"], E.cnt["d_dbg"]))
        E.barrier()
        E.flush(block)
        p1.close()
        if stop_after == "A1":
            return nc

        p2 = ExitStack()
        P = p2.enter_context
        W1TH = P(nc.sbuf_tensor("W1TH", [128, 32 * 1024], BF16))
        CLD = [P(nc.sbuf_tensor(f"CLD{i}", [128, CTX], BF16)) for i in range(2)]
        CLT = P(nc.sbuf_tensor("CLT", [128, CTX], BF16))
        H1B = P(nc.sbuf_tensor("H1B", [128, 1024], F32))
        GLO = [P(nc.sbuf_tensor(f"GLO{i}", [128, 1024], BF16)) for i in range(2)]
        B1REP = P(nc.sbuf_tensor("B1REP", [128, H1], F32))
        B1ROW = P(nc.sbuf_tensor("B1ROW", [1, H1 + 1024], F32))
        WQS = P(nc.sbuf_tensor("WQS", [128, 1024], F32))
        WQI = P(nc.sbuf_tensor("WQI", [128, 1024], I32))
        WQO = P(nc.sbuf_tensor("WQO", [128, 1024], BF16))
        p_ct = P(nc.psum_tensor("p_ct", [128, 2048], BF16))
        p_h1 = P(nc.psum_tensor("p_h1", [128, 1024], F32))
        p_b1 = P(nc.psum_tensor("p_b1", [1, 1024], F32))
        p_bc2 = P(nc.psum_tensor("p_bc2", [128, 1024], F32))

        e_rstdc = rstd_batch(VARC, RSTDC, e_varc)
        e_b1ld = E.dma("sync", B1ROW[0:1, 0:H1], d_b1, slot="b1l")

        qchunks = []
        for (src, dst, nch, w) in ((d_wg, d_qwg, 22, D), (d_wu, d_qwu, 22, D)):
            for c in range(nch):
                qchunks.append((src, dst, c, w))
        for c in range(8):
            qchunks.append((d_wd, d_qwd, c, IFF))
        qspill = {"last_g": None, "last_u": None, "last_d": None}
        qprev = [None, None]

        def emit_quant_chunk(idx):
            src, dst, c, w = qchunks[idx]
            if w == D:
                stage = WQS
                ld = E.dma("sync", stage[:, 0:w], src[_ts(c), :],
                           deps=[qprev[0]] if qprev[0] else (), slot="qs")
            else:
                stage = TB[0]
                ld = E.dma("sync", stage[:, 0:w], src[_ts(c, 128), :],
                           deps=[qprev[0]] if qprev[0] else (), slot="qs")
            e_am = E.op("vector", lambda v, stage=stage, w=w: v.reduce_max(
                STT[:, 0:1], stage[:, 0:w], axis=AX, apply_absolute_value=True),
                deps=[ld], inc=1)
            e_s = E.op("vector", lambda v: v.tensor_scalar(
                STT[:, 1:2], STT[:, 0:1], 1.0 / 127.0, None, ALU.mult),
                deps=[e_am], inc=1)
            e_r = E.op("vector", lambda v: v.reciprocal(STT[:, 2:3], STT[:, 1:2]),
                       deps=[e_s], inc=1)
            spills = []
            nsub = (w + 1023) // 1024
            for sub in range(nsub):
                lo = sub * 1024
                hi = min(w, lo + 1024)
                sw = hi - lo
                e_t = E.op("vector", lambda v, stage=stage, lo=lo, sw=sw: v.tensor_scalar(
                    WQI[:, 0:sw],
                    stage[:, lo:lo + sw], STT[:, 2:3], None, ALU.mult),
                    deps=[e_r] + ([qprev[0]] if qprev[0] else []) +
                         ([qprev[1]] if qprev[1] else []), inc=1)
                e_qi = E.op("vector", lambda v, sw=sw: v.tensor_scalar(
                    WQI[:, 0:sw], WQI[:, 0:sw], 127.0, -128.0, ALU.min, ALU.max),
                    deps=[e_t], inc=1)
                e_qo = E.op("vector", lambda v, sw=sw: v.tensor_scalar(
                    WQO[:, 0:sw], WQI[:, 0:sw], STT[:, 1:2], None, ALU.mult),
                    deps=[e_qi] + ([qprev[0]] if qprev[0] else []), inc=1)
                qprev[1] = e_qo
                if w == D:
                    sp = E.dma("gpsimd", dst[_ts(c), :], WQO[:, 0:sw], deps=[e_qo],
                               slot="spq")
                else:
                    sp = E.dma("gpsimd", dst[_ts(c, 128), lo:lo + sw], WQO[:, 0:sw],
                               deps=[e_qo], slot="spq")
                spills.append(sp)
                qprev[0] = sp
            if dst is d_qwg and c == 21:
                qspill["last_g"] = spills[-1]
            if dst is d_qwu and c == 21:
                qspill["last_u"] = spills[-1]
            if dst is d_qwd and c == 7:
                qspill["last_d"] = spills[-1]

        gl_dsts = (d_gla, d_glb)
        a2taps = []
        prev_w1_consumed = []
        cld_free = [None, None]
        glo_free = [None, None]
        prev_ev = {0: None, 1: None}
        prev_te = {0: None, 1: None}
        prev_gelu = None
        last_gl_dma = None
        qemit = 0
        for half in range(2):
            w1_lds = []
            for k in range(32):
                w1_lds.append(E.dmat(
                    "scalar", W1TH[:, k * 1024:(k + 1) * 1024],
                    d_w1bf[half * 1024:(half + 1) * 1024, k * 128:(k + 1) * 128],
                    deps=([ev_w1bf] if half == 0 and k == 0 else []) +
                         (list(prev_w1_consumed) if k == 0 else []),
                    slot="w1l"))
            b1mm = None
            for k in range(32):
                for n in range(2):
                    b1mm = E.op("tensor", lambda t, k=k, n=n: t.matmul(
                        p_b1[0:1, n * 512:(n + 1) * 512], BETA_TB[:, k:k + 1],
                        W1TH[:, k * 1024 + n * 512: k * 1024 + (n + 1) * 512],
                        start=(k == 0), stop=(k == 31)),
                        deps=[w1_lds[-1], e_beta] if (k == 0 and n == 0) else [],
                        inc=(1 if (k == 31 and n == 1) else 0))
            e_b1r = E.op("vector", lambda v, half=half: v.tensor_add(
                B1ROW[0:1, 2048:3072], p_b1[0:1, 0:1024],
                B1ROW[0:1, half * 1024:(half + 1) * 1024]),
                deps=[b1mm, e_b1ld], inc=1)
            b1revs = []
            for n in range(2):
                mm = E.op("tensor", lambda t, n=n: t.matmul(
                    p_bc2[:, _ts(n, 512)], ONESF[0:1, :],
                    B1ROW[0:1, 2048 + n * 512:2048 + (n + 1) * 512],
                    start=True, stop=True), deps=[e_b1r], inc=1)
                b1revs.append(E.op("vector", lambda v, n=n, half=half: v.tensor_copy(
                    B1REP[:, half * 1024 + n * 512: half * 1024 + (n + 1) * 512],
                    p_bc2[:, _ts(n, 512)]), deps=[mm], inc=1))

            for j in range(NT):
                b = j % 2
                ld = E.dma("sync", CLD[b][:], d_ctxc[_ts(j), :],
                           deps=([cld_free[b]] if cld_free[b] else []) + a2taps,
                           slot=f"cl{b}")
                a2taps = []
                bevs = []
                for bb in range(4):
                    par = bb % 2
                    tev = None
                    for c in range(8):
                        cc = bb * 8 + c
                        tev = E.op("tensor", lambda t, b=b, cc=cc, par=par, c=c: t.transpose(
                            p_ct[:, par * 1024 + c * 128: par * 1024 + (c + 1) * 128],
                            CLD[b][:, _ts(cc)], IDB[:]),
                            deps=[ld, prev_te[par]] if c == 0 else [],
                            inc=(1 if c == 7 else 0))
                    bevs.append(E.op("vector", lambda v, bb=bb, par=par: v.tensor_copy(
                        CLT[:, bb * 1024:(bb + 1) * 1024],
                        p_ct[:, par * 1024:par * 1024 + 1024]),
                        deps=[tev], inc=1))
                    prev_te[par] = bevs[-1]
                cld_free[b] = bevs[-1]
                for n in range(2):
                    lastm = None
                    for k in range(32):
                        lastm = E.op("tensor", lambda t, n=n, k=k: t.matmul(
                            p_h1[:, _ts(n, 512)], CLT[:, _ts(k)],
                            W1TH[:, k * 1024 + n * 512: k * 1024 + (n + 1) * 512],
                            start=(k == 0), stop=(k == 31)),
                            deps=(bevs + [w1_lds[-1]] +
                                  ([prev_ev[n]] if prev_ev[n] else [])) if k == 0 else [],
                            inc=(1 if k == 31 else 0))
                    prev_ev[n] = E.op("vector", lambda v, n=n, j=j, half=half: v.scalar_tensor_tensor(
                        H1B[:, _ts(n, 512)], p_h1[:, _ts(n, 512)], RSTDC[:, j:j + 1],
                        B1REP[:, half * 1024 + n * 512: half * 1024 + (n + 1) * 512],
                        ALU.mult, ALU.add),
                        deps=[lastm, e_rstdc] + ([prev_gelu] if prev_gelu else []) +
                             (b1revs if j == 0 else []), inc=1)
                e_gl = E.op("scalar", lambda s, b=b: s.activation(
                    GLO[b][:], H1B[:], AF.Gelu),
                    deps=[prev_ev[0], prev_ev[1]] +
                         ([glo_free[b]] if glo_free[b] else []), inc=1)
                prev_gelu = e_gl
                last_gl_dma = E.dma("gpsimd", gl_dsts[half][_ts(j), :], GLO[b][:],
                                    deps=[e_gl], slot=f"sg{b}")
                glo_free[b] = last_gl_dma
                if "h1" in tap_d and half == 0:
                    a2taps.append(E.dma("gpsimd", tap_d["h1"][_ts(j), :], H1B[:],
                          deps=[prev_ev[0], prev_ev[1], e_gl], slot="tap"))
                if "gl" in tap_d and half == 0:
                    a2taps.append(E.dma("gpsimd", tap_d["gl"][_ts(j), :], GLO[b][:],
                                        deps=[e_gl], slot="tap"))
                while qemit < len(qchunks) and qemit < (half * NT + j + 1):
                    emit_quant_chunk(qemit)
                    qemit += 1
            prev_w1_consumed = [prev_ev[0], prev_ev[1]]
        while qemit < len(qchunks):
            emit_quant_chunk(qemit)
            qemit += 1

        E.barrier()
        E.flush(block)
        p2.close()
        if stop_after == "A2a":
            return nc

        p3 = ExitStack()
        P = p3.enter_context
        W2T = P(nc.sbuf_tensor("W2T", [128, 16 * 1024], BF16))
        GLD = [P(nc.sbuf_tensor(f"GLD{i}", [128, H1], BF16)) for i in range(2)]
        GLT = P(nc.sbuf_tensor("GLT", [128, H1], BF16))
        B2B = P(nc.sbuf_tensor("B2B", [1, D + 1024], F32))
        B2BB = P(nc.sbuf_tensor("B2BB", [1, D], BF16))
        p_gt = P(nc.psum_tensor("p_gt", [128, 2048], BF16))
        p_h2 = P(nc.psum_tensor("p_h2", [128, 1024], F32))

        w2_lds = []
        for k in range(16):
            w2_lds.append(E.dmat("sync", W2T[:, k * 1024:(k + 1) * 1024],
                                 d_w2bf[0:D, k * 128:(k + 1) * 128],
                                 deps=[ev_w2bf] if k == 0 else (), slot="w2l"))
        e_b2ld = E.dma("sync", B2B[0:1, 0:D], d_b2, slot="b2l")
        e_b2b = E.op("vector", lambda v: v.tensor_copy(B2BB[0:1, :], B2B[0:1, 0:D]),
                     deps=[e_b2ld], inc=1)
        e_amz = E.op("vector", lambda v: v.memset(AMX[:, 0:2], 0.0), inc=1)
        prev_am = e_amz
        amax_par = 0
        gld_free = [None, None]
        prev_gt = {0: None, 1: None}
        prev_h2 = None
        xm_wfree = [[], []]
        prev_scr2 = None
        xmid_dmas = []
        bttaps = []
        for j in range(NT):
            b = j % 2
            ldg0 = E.dma("sync", GLD[b][:, 0:1024], d_gla[_ts(j), :],
                         deps=([gld_free[b]] if gld_free[b] else []) + bttaps,
                         slot=f"ga{b}")
            bttaps = []
            ldg1 = E.dma("sync", GLD[b][:, 1024:2048], d_glb[_ts(j), :],
                         deps=[gld_free[b]] if gld_free[b] else (), slot=f"gb{b}")
            ldx = E.dma("sync", XB[b][:], d_x[_ts(j), :],
                        deps=(x_free[b] if isinstance(x_free[b], list)
                              else [x_free[b]]) if x_free[b] else (), slot=f"x{b}")
            gevs = []
            for bb in range(2):
                tev = None
                for c in range(8):
                    cc = bb * 8 + c
                    tev = E.op("tensor", lambda t, b=b, cc=cc, bb=bb, c=c: t.transpose(
                        p_gt[:, bb * 1024 + c * 128: bb * 1024 + (c + 1) * 128],
                        GLD[b][:, _ts(cc)], IDB[:]),
                        deps=[ldg0, ldg1, prev_gt[bb]] if c == 0 else [],
                        inc=(1 if c == 7 else 0))
                gevs.append(E.op("vector", lambda v, bb=bb: v.tensor_copy(
                    GLT[:, bb * 1024:(bb + 1) * 1024],
                    p_gt[:, bb * 1024:bb * 1024 + 1024]),
                    deps=[tev], inc=1))
                prev_gt[bb] = gevs[-1]
            gld_free[b] = gevs[-1]
            for n in range(2):
                for k in range(16):
                    E.op("tensor", lambda t, n=n, k=k: t.matmul(
                        p_h2[:, _ts(n, 512)], GLT[:, _ts(k)],
                        W2T[:, k * 1024 + n * 512: k * 1024 + (n + 1) * 512],
                        start=(k == 0), stop=False),
                        deps=(gevs + [w2_lds[-1]] +
                              ([prev_h2] if prev_h2 else [])) if (n == 0 and k == 0) else [])
            lastm = None
            for n in range(2):
                lastm = E.op("tensor", lambda t, n=n: t.matmul(
                    p_h2[:, _ts(n, 512)], ONESB[0:1, :],
                    B2BB[0:1, n * 512:(n + 1) * 512],
                    start=False, stop=True),
                    deps=[e_b2b] if n == 0 else [], inc=(1 if n == 1 else 0))
            e_x1 = E.op("vector", lambda v, b=b, j=j: v.tensor_scalar(
                X1[:], XB[b][:], MUX[:, j:j + 1], RSTDX[:, j:j + 1],
                ALU.subtract, ALU.mult), deps=[ldx], inc=1)
            e_xm1 = E.op("vector", lambda v, b=b: v.tensor_add(X2S[:], XB[b][:], X1[:]),
                         deps=[e_x1] + ([xm_free[b]] if xm_free[b] else []), inc=1)
            x_free[b] = e_xm1
            exm = []
            for n in range(2):
                exm.append(E.op("vector", lambda v, b=b, n=n: v.tensor_add(
                    XM[b][:, _ts(n, 512)], X2S[:, _ts(n, 512)], p_h2[:, _ts(n, 512)]),
                    deps=[lastm] + (xm_wfree[b] if n == 0 else []), inc=1))
            prev_h2 = exm[-1]
            xmid_dmas.append(E.dma("gpsimd", d_xmid[_ts(j), :], XM[b][:], deps=exm,
                                   slot=f"sx{b}"))
            e1 = E.op("vector", lambda v, b=b: v.reduce_sum(STT[:, 0:1], XM[b][:], axis=AX),
                      deps=exm, inc=1)
            e_mu = E.op("vector", lambda v, j=j: v.tensor_scalar(
                MU2[:, j:j + 1], STT[:, 0:1], 1.0 / D, None, ALU.mult),
                deps=[e1], inc=1)
            e_sq2 = E.op("scalar", lambda s, b=b: s.activation(
                SCR[:], XM[b][:], AF.Square),
                deps=exm + ([prev_scr2] if prev_scr2 else []), inc=1)
            e2 = E.op("vector", lambda v: v.reduce_sum(STT[:, 1:2], SCR[:], axis=AX),
                      deps=[e_sq2], inc=1)
            prev_scr2 = e2
            xm_wfree[b] = [xmid_dmas[-1], e_sq2]
            e3 = E.op("vector", lambda v, j=j: v.tensor_mul(
                STT[:, 2:3], MU2[:, j:j + 1], MU2[:, j:j + 1]), deps=[e2], inc=1)
            e4 = E.op("vector", lambda v: v.tensor_scalar(
                STT[:, 3:4], STT[:, 1:2], 1.0 / D, None, ALU.mult), deps=[e2], inc=1)
            e5 = E.op("vector", lambda v: v.tensor_sub(STT[:, 4:5], STT[:, 3:4],
                                                       STT[:, 2:3]),
                      deps=[e3, e4], inc=1)
            e6 = E.op("vector", lambda v: v.tensor_scalar(
                STT[:, 5:6], STT[:, 4:5], LN_EPS, None, ALU.add), deps=[e5], inc=1)
            e7 = E.op("scalar", lambda s, j=j: s.activation(
                RSTD2[:, j:j + 1], STT[:, 5:6], AF.Ln), deps=[e6], inc=1)
            e8 = E.op("scalar", lambda s, j=j: s.activation(
                RSTD2[:, j:j + 1], RSTD2[:, j:j + 1], AF.Exp, scale=-0.5),
                deps=[e7], inc=1)
            e9 = E.op("vector", lambda v, j=j: v.tensor_mul(
                STT[:, 6:7], RSTD2[:, j:j + 1], RSTD2[:, j:j + 1]), deps=[e8], inc=1)
            e10 = E.op("vector", lambda v: v.tensor_mul(STT[:, 6:7], STT[:, 6:7],
                                                        STT[:, 5:6]), deps=[e9], inc=1)
            e11 = E.op("vector", lambda v: v.tensor_scalar(
                STT[:, 6:7], STT[:, 6:7], -0.5, 1.5, ALU.mult, ALU.add),
                deps=[e10], inc=1)
            e12 = E.op("vector", lambda v, j=j: v.tensor_mul(
                RSTD2[:, j:j + 1], RSTD2[:, j:j + 1], STT[:, 6:7]), deps=[e11], inc=1)
            e_x2 = E.op("vector", lambda v, b=b, j=j: v.tensor_scalar(
                X2S[:], XM[b][:], MU2[:, j:j + 1], RSTD2[:, j:j + 1],
                ALU.subtract, ALU.mult), deps=[e12], inc=1)
            xm_free[b] = e_x2
            e_am = E.op("vector", lambda v: v.reduce_max(
                STT[:, 7:8], X2S[:], axis=AX, apply_absolute_value=True),
                deps=[e_x2], inc=1)
            prev_am = E.op("vector", lambda v, ap=amax_par: v.tensor_max(
                AMX[:, 1 - ap:2 - ap], AMX[:, ap:ap + 1], STT[:, 7:8]),
                deps=[e_am, prev_am], inc=1)
            amax_par = 1 - amax_par
            if "xmid" in tap_d:
                bttaps.append(E.dma("gpsimd", tap_d["xmid"][_ts(j), :], XM[b][:],
                                    deps=[e_x2] + exm, slot="tap"))
        e_rstd2_last = e12

        E.barrier()
        E.flush(block)
        p3.close()
        if stop_after == "A2b":
            return nc

        p4 = ExitStack()
        P = p4.enter_context
        p_sc = P(nc.psum_tensor("p_sc", [128, 128], F32))
        ev = E.op("tensor", lambda t, ap=amax_par: t.transpose(
            p_sc[0:1, 0:128], AMX[:, ap:ap + 1], IDF[:]), deps=[prev_am], inc=1)
        e_row = E.op("vector", lambda v: v.tensor_copy(CCROW[0:1, :], p_sc[0:1, 0:128]),
                     deps=[ev], inc=1)
        e_w = E.dma("gpsimd", cc1i.ap(), CCROW[0:1, :], deps=[e_row], slot="ccd")
        e_cc = E.op("gpsimd", lambda gg: gg.collective_compute(
            "AllReduce", ALU.max, replica_groups=[list(range(NCORE))],
            ins=[cc1i.ap()], outs=[cc1o.ap()]), deps=[e_w], sem="s_cc", inc=1,
            async_ev=True)
        e_r = E.dma("gpsimd", CCROW[0:1, :], cc1o.ap(), deps=[e_cc], slot="ccd")
        e_gm = E.op("vector", lambda v: v.reduce_max(SCAL[0:1, 0:1], CCROW[0:1, :],
                                                     axis=AX), deps=[e_r], inc=1)
        e_sx = E.op("vector", lambda v: v.tensor_mul(SCAL[0:1, 1:2], SCAL[0:1, 0:1],
                                                     SCAL[0:1, 10:11]),
                    deps=[e_gm, e_sgx], inc=1)
        e_sx2 = E.op("vector", lambda v: v.tensor_scalar(
            SCAL[0:1, 1:2], SCAL[0:1, 1:2], 1.0 / 127.0, None, ALU.mult),
            deps=[e_sx], inc=1)
        e_isx = E.op("vector", lambda v: v.reciprocal(SCAL[0:1, 2:3], SCAL[0:1, 1:2]),
                     deps=[e_sx2], inc=1)
        bc1 = []
        prevcp = e_isx
        for (col, dst) in ((1, SXCOL), (2, INVSX)):
            mm = E.op("tensor", lambda t, col=col: t.matmul(
                p_sc[:, 0:1], ONESF[0:1, :], SCAL[0:1, col:col + 1],
                start=True, stop=True), deps=[e_isx, prevcp], inc=1)
            prevcp = E.op("vector", lambda v, dst=dst: v.tensor_copy(
                dst[:], p_sc[:, 0:1]), deps=[mm], inc=1)
            bc1.append(prevcp)
        e_r2s = E.op("vector", lambda v: v.tensor_scalar(
            RSTD2S[:, 0:NT], RSTD2[:, 0:NT], INVSX[:], None, ALU.mult),
            deps=bc1 + [e_rstd2_last], inc=1)

        E.barrier()
        E.flush(block)
        p4.close()
        if stop_after == "CC1":
            return nc

        p5 = ExitStack()
        P = p5.enter_context
        WGT = P(nc.sbuf_tensor("WGT", [128, 8 * IFF], BF16))
        WUT = P(nc.sbuf_tensor("WUT", [128, 8 * IFF], BF16))
        QI = P(nc.sbuf_tensor("QI", [128, D], I32))
        QB = P(nc.sbuf_tensor("QB", [128, D], BF16))
        QT = P(nc.sbuf_tensor("QT", [128, D], BF16))
        SILU = P(nc.sbuf_tensor("SILU", [128, 1024], F32))
        UPS = P(nc.sbuf_tensor("UPS", [128, 1024], F32))
        p_qt = P(nc.psum_tensor("p_qt", [128, 1024], BF16))
        p_g = P(nc.psum_tensor("p_g", [128, 1024], F32))
        p_u = P(nc.psum_tensor("p_u", [128, 1024], F32))

        wg_lds, wu_lds = [], []
        for c in range(8):
            wg_lds.append(E.dmat("sync", WGT[:, c * IFF:(c + 1) * IFF],
                                 d_qwg[0:IFF, c * 128:(c + 1) * 128],
                                 deps=[qspill["last_g"]] if c == 0 else (),
                                 slot="wgl"))
        for c in range(8):
            wu_lds.append(E.dmat("sync", WUT[:, c * IFF:(c + 1) * IFF],
                                 d_qwu[0:IFF, c * 128:(c + 1) * 128],
                                 deps=[qspill["last_u"]] if c == 0 else (),
                                 slot="wul"))
        NCH = [(0, 512), (512, 512), (1024, 512), (1536, 512), (2048, 512),
               (2560, 256)]
        e_amz2 = E.op("vector", lambda v: v.memset(AMT[:, 0:2], 0.0), inc=1)
        prev_amt = e_amz2
        tpar = 0
        prev_qt = None
        prev_qb = None
        prev_sil = {0: None, 1: None}
        prev_ups = {0: None, 1: None}
        prev_tmul = {0: None, 1: None}
        t_dmas = []
        btaps = []
        for j in range(NT):
            b = j % 2
            ldm = E.dma("sync", XM[b][:], d_xmid[_ts(j), :],
                        deps=[xmid_dmas[j]] + ([xm_free[b]] if xm_free[b] else []) +
                             btaps,
                        slot=f"xm{b}")
            btaps = []
            e_z = E.op("vector", lambda v, b=b, j=j: v.tensor_scalar(
                X2S[:], XM[b][:], MU2[:, j:j + 1], RSTD2S[:, j:j + 1],
                ALU.subtract, ALU.mult), deps=[ldm, e_r2s], inc=1)
            xm_free[b] = e_z
            e_qi = E.op("vector", lambda v: v.tensor_scalar(
                QI[:], X2S[:], 127.49, -128.49, ALU.min, ALU.max),
                deps=[e_z] + ([prev_qb] if prev_qb else []), inc=1)
            e_qb = E.op("scalar", lambda s: s.activation(QB[:], QI[:], AF.Copy),
                        deps=[e_qi] + ([prev_qt] if prev_qt else []), inc=1)
            prev_qb = e_qb
            tev = None
            for c in range(8):
                tev = E.op("tensor", lambda t, c=c: t.transpose(
                    p_qt[:, _ts(c)], QB[:, _ts(c)], IDB[:]),
                    deps=[e_qb] + ([prev_qt] if prev_qt else []) if c == 0 else [],
                    inc=(1 if c == 7 else 0))
            e_qt = E.op("vector", lambda v: v.tensor_copy(QT[:], p_qt[:, 0:1024]),
                        deps=[tev], inc=1)
            prev_qt = e_qt
            if "q" in tap_d:
                btaps.append(E.dma("gpsimd", tap_d["q"][_ts(j), :], QB[:],
                                   deps=[e_qb, tev], slot="tap"))
            last_t = None
            for ci, (off, w) in enumerate(NCH):
                par = ci % 2
                lastg = None
                for k in range(8):
                    lastg = E.op("tensor", lambda t, k=k, off=off, w=w, par=par: t.matmul(
                        p_g[:, par * 512:par * 512 + w], QT[:, _ts(k)],
                        WGT[:, k * IFF + off: k * IFF + off + w],
                        start=(k == 0), stop=(k == 7)),
                        deps=([e_qt] + ([prev_sil[par]] if prev_sil[par] else []) +
                              ([wg_lds[-1]] if j == 0 else [])) if k == 0 else [],
                        inc=(1 if k == 7 else 0))
                lastu = None
                for k in range(8):
                    lastu = E.op("tensor", lambda t, k=k, off=off, w=w, par=par: t.matmul(
                        p_u[:, par * 512:par * 512 + w], QT[:, _ts(k)],
                        WUT[:, k * IFF + off: k * IFF + off + w],
                        start=(k == 0), stop=(k == 7)),
                        deps=(([prev_ups[par]] if prev_ups[par] else []) +
                              ([wu_lds[-1]] if j == 0 else [])) if k == 0 else [],
                        inc=(1 if k == 7 else 0))
                e_sil = E.op("scalar", lambda s, w=w, par=par: s.activation(
                    SILU[:, par * 512:par * 512 + w], p_g[:, par * 512:par * 512 + w],
                    AF.Silu, scale=SXCOL[:]),
                    deps=[lastg] + ([prev_tmul[par]] if prev_tmul[par] else []) +
                         (bc1 if j == 0 else []), inc=1)
                prev_sil[par] = e_sil
                e_up = E.op("scalar", lambda s, w=w, par=par: s.activation(
                    UPS[:, par * 512:par * 512 + w], p_u[:, par * 512:par * 512 + w],
                    AF.Copy, scale=SXCOL[:]), deps=[lastu], inc=1)
                prev_ups[par] = e_up
                last_t = E.op("vector", lambda v, b=b, off=off, w=w, par=par: v.tensor_mul(
                    TB[b][:, off:off + w], SILU[:, par * 512:par * 512 + w],
                    UPS[:, par * 512:par * 512 + w]),
                    deps=[e_sil, e_up] +
                         ([tb_free[b]] if (ci == 0 and tb_free[b]) else []), inc=1)
                prev_tmul[par] = last_t
            e_amt = E.op("vector", lambda v, b=b: v.reduce_max(
                STT[:, 7:8], TB[b][:], axis=AX, apply_absolute_value=True),
                deps=[last_t], inc=1)
            prev_amt = E.op("vector", lambda v, tp=tpar: v.tensor_max(
                AMT[:, 1 - tp:2 - tp], AMT[:, tp:tp + 1], STT[:, 7:8]),
                deps=[e_amt, prev_amt], inc=1)
            tpar = 1 - tpar
            t_dmas.append(E.dma("gpsimd", d_t[_ts(j), :], TB[b][:], deps=[last_t],
                                slot=f"st{b}"))
            tb_free[b] = t_dmas[-1]
            if "t" in tap_d:
                btaps.append(E.dma("gpsimd", tap_d["t"][_ts(j), :], TB[b][:],
                                   deps=[last_t], slot="tap"))

        E.barrier()
        E.flush(block)
        p5.close()
        if stop_after == "B":
            return nc

        p6 = ExitStack()
        P = p6.enter_context
        p_sc2 = P(nc.psum_tensor("p_sc2", [128, 128], F32))
        ev = E.op("tensor", lambda t, tp=tpar: t.transpose(
            p_sc2[0:1, 0:128], AMT[:, tp:tp + 1], IDF[:]), deps=[prev_amt], inc=1)
        e_row2 = E.op("vector", lambda v: v.tensor_copy(CCROW[0:1, :],
                                                        p_sc2[0:1, 0:128]),
                      deps=[ev], inc=1)
        e_w2 = E.dma("gpsimd", cc2i.ap(), CCROW[0:1, :], deps=[e_row2], slot="ccd")
        e_cc2 = E.op("gpsimd", lambda gg: gg.collective_compute(
            "AllReduce", ALU.max, replica_groups=[list(range(NCORE))],
            ins=[cc2i.ap()], outs=[cc2o.ap()]), deps=[e_w2], sem="s_cc", inc=1,
            async_ev=True)
        e_r2 = E.dma("gpsimd", CCROW[0:1, :], cc2o.ap(), deps=[e_cc2], slot="ccd")
        e_gm2 = E.op("vector", lambda v: v.reduce_max(SCAL[0:1, 4:5], CCROW[0:1, :],
                                                      axis=AX), deps=[e_r2], inc=1)
        e_sd = E.op("vector", lambda v: v.tensor_mul(SCAL[0:1, 5:6], SCAL[0:1, 4:5],
                                                     SCAL[0:1, 11:12]),
                    deps=[e_gm2, e_sgd], inc=1)
        e_sd2 = E.op("vector", lambda v: v.tensor_scalar(
            SCAL[0:1, 5:6], SCAL[0:1, 5:6], 1.0 / 127.0, None, ALU.mult),
            deps=[e_sd], inc=1)
        e_isd = E.op("vector", lambda v: v.reciprocal(SCAL[0:1, 6:7], SCAL[0:1, 5:6]),
                     deps=[e_sd2], inc=1)
        bc2 = []
        prevcp2 = e_isd
        for (col, dst) in ((5, SDCOL), (6, INVSD)):
            mm = E.op("tensor", lambda t, col=col: t.matmul(
                p_sc2[:, 0:1], ONESF[0:1, :], SCAL[0:1, col:col + 1],
                start=True, stop=True), deps=[e_isd, prevcp2], inc=1)
            prevcp2 = E.op("vector", lambda v, dst=dst: v.tensor_copy(
                dst[:], p_sc2[:, 0:1]), deps=[mm], inc=1)
            bc2.append(prevcp2)
        if "scal" in tap_d:
            e_dbg1 = E.op("vector", lambda v: v.tensor_copy(SCAL[0:1, 12:13], SXCOL[0:1, 0:1]),
                          deps=bc1, inc=1)
            e_dbg2 = E.op("vector", lambda v: v.tensor_copy(SCAL[0:1, 13:14], INVSX[0:1, 0:1]),
                          deps=bc1, inc=1)
            E.dma("gpsimd", tap_d["scal"], SCAL[0:1, 0:16],
                  deps=[e_isd, e_dbg1, e_dbg2] + bc2, slot="tap")

        E.barrier()
        E.flush(block)
        p6.close()
        if stop_after == "CC2":
            return nc

        p7 = ExitStack()
        P = p7.enter_context
        WDT = P(nc.sbuf_tensor("WDT", [128, 22 * 1024], BF16))
        Q2S = P(nc.sbuf_tensor("Q2S", [128, IFF], F32))
        Q2I = P(nc.sbuf_tensor("Q2I", [128, IFF], I32))
        Q2B = P(nc.sbuf_tensor("Q2B", [128, IFF], BF16))
        Q2T = P(nc.sbuf_tensor("Q2T", [128, IFF], BF16))
        OB = [P(nc.sbuf_tensor(f"OB{i}", [128, D], F32)) for i in range(2)]
        p_q2 = P(nc.psum_tensor("p_q2", [128, 2048], BF16))
        p_dn = P(nc.psum_tensor("p_dn", [128, 1024], F32))

        wd_lds = []
        for c in range(22):
            wd_lds.append(E.dmat("sync", WDT[:, c * 1024:(c + 1) * 1024],
                                 d_qwd[0:D, c * 128:(c + 1) * 128],
                                 deps=[qspill["last_d"]] if c == 0 else (),
                                 slot="wdl"))
        ob_free = [None, None]
        prev_q2t = {0: None, 1: None}
        prev_q2b = None
        prev_dn = {0: None, 1: None}
        ctaps = []
        for j in range(NT):
            b = j % 2
            ldt = E.dma("sync", TB[b][:], d_t[_ts(j), :],
                        deps=[t_dmas[j]] + ([tb_free[b]] if tb_free[b] else []) +
                             ctaps,
                        slot=f"tl{b}")
            ctaps = []
            ldxm = E.dma("sync", XM[b][:], d_xmid[_ts(j), :],
                         deps=[xm_free[b]] if xm_free[b] else (), slot=f"xm{b}")
            e_z2 = E.op("vector", lambda v, b=b: v.tensor_scalar(
                Q2S[:], TB[b][:], INVSD[:], None, ALU.mult),
                deps=[ldt] + (bc2 if j == 0 else []), inc=1)
            tb_free[b] = e_z2
            e_q2i = E.op("vector", lambda v: v.tensor_scalar(
                Q2I[:], Q2S[:], 127.49, -128.49, ALU.min, ALU.max),
                deps=[e_z2] + ([prev_q2b] if prev_q2b else []), inc=1)
            e_q2b = E.op("scalar", lambda s: s.activation(Q2B[:], Q2I[:], AF.Copy),
                         deps=[e_q2i, prev_q2t[0], prev_q2t[1]], inc=1)
            prev_q2b = e_q2b
            if "q2" in tap_d:
                ctaps.append(E.dma("gpsimd", tap_d["q2"][_ts(j), :], Q2B[:],
                                   deps=[e_q2b], slot="tap"))
            bevs = []
            for bi, (c0, cnt) in enumerate(((0, 8), (8, 8), (16, 6))):
                par = bi % 2
                tev = None
                for c in range(cnt):
                    cc = c0 + c
                    tev = E.op("tensor", lambda t, cc=cc, c=c, par=par: t.transpose(
                        p_q2[:, par * 1024 + c * 128: par * 1024 + (c + 1) * 128],
                        Q2B[:, _ts(cc)], IDB[:]),
                        deps=[e_q2b] + ([prev_q2t[par]] if prev_q2t[par] else [])
                        if c == 0 else [],
                        inc=(1 if c == cnt - 1 else 0))
                bevs.append(E.op("vector", lambda v, c0=c0, cnt=cnt, par=par: v.tensor_copy(
                    Q2T[:, c0 * 128:(c0 + cnt) * 128],
                    p_q2[:, par * 1024:par * 1024 + cnt * 128]),
                    deps=[tev], inc=1))
                prev_q2t[par] = bevs[-1]
            for n in range(2):
                lastm = None
                for k in range(22):
                    lastm = E.op("tensor", lambda t, n=n, k=k: t.matmul(
                        p_dn[:, _ts(n, 512)], Q2T[:, _ts(k)],
                        WDT[:, k * 1024 + n * 512: k * 1024 + (n + 1) * 512],
                        start=(k == 0), stop=(k == 21)),
                        deps=(bevs + ([prev_dn[n]] if prev_dn[n] else []) +
                              ([wd_lds[-1]] if j == 0 else [])) if k == 0 else [],
                        inc=(1 if k == 21 else 0))
                prev_dn[n] = E.op("vector", lambda v, b=b, n=n: v.scalar_tensor_tensor(
                    OB[b][:, _ts(n, 512)], p_dn[:, _ts(n, 512)], SDCOL[:],
                    XM[b][:, _ts(n, 512)], ALU.mult, ALU.add),
                    deps=[lastm, ldxm] + bc2 +
                         ([ob_free[b]] if (n == 0 and ob_free[b]) else []), inc=1)
            xm_free[b] = prev_dn[1]
            ob_free[b] = E.dma("sync", d_y[_ts(j), :], OB[b][:],
                               deps=[prev_dn[0], prev_dn[1]], slot=f"o{b}")

        for sl in ("d_o0", "d_o1") + (("d_tap",) if tap_d else ()):
            if sl in E.sems:
                E.op("sync", lambda e, h=E.sems[sl], v=E.cnt[sl]: e.wait_ge(h, v))
        E.barrier()
        E.flush(block)
        p7.close()

    return nc


def host_constants():
    bf = ml_dtypes.bfloat16
    idb = np.eye(128, dtype=bf)
    idf = np.eye(128, dtype=np.float32)
    ltri = np.triu(np.ones((128, 128), dtype=np.float32)).astype(bf)
    onesb = np.ones((1, 128), dtype=bf)
    onesf = np.ones((1, 128), dtype=np.float32)
    cmw = np.zeros((1, TSEQ * 128), dtype=np.float32)
    for jj in range(TSEQ):
        cmw[0, jj * 128:(jj + 1) * 128] = 128.0 * jj
    cmw = cmw.astype(bf)
    pos = np.arange(1, TSEQ * 128 + 1, dtype=np.float64).reshape(TSEQ, 128)
    posc = (1.0 / np.sqrt(pos)).astype(np.float32)
    invp = (1.0 / pos).astype(np.float32)
    cons = np.zeros((128, 4), dtype=np.float32)
    cons[:, 0] = HALF_PI
    cons[:, 1] = -1.0
    cons[:, 2] = -0.5
    return dict(idb=idb, idf=idf, ltri=ltri, onesb=onesb, onesf=onesf,
                cmw=cmw, posc=posc, invp=invp, cons=cons)


def make_in_maps(inputs):
    c = host_constants()
    x = np.ascontiguousarray(np.asarray(inputs["L0_attn_x"], dtype=np.float32))
    f32 = np.float32
    base = {
        "W_omega": np.ascontiguousarray(inputs["L0_attn_W_omega"], dtype=f32),
        "b_omega": np.ascontiguousarray(inputs["L0_attn_b_omega"], dtype=f32).reshape(1, D),
        "log_scale": np.ascontiguousarray(inputs["L0_attn_log_scale"], dtype=f32).reshape(1, D),
        "ln_gamma": np.ascontiguousarray(inputs["L0_attn_ln_gamma"], dtype=f32).reshape(1, CTX),
        "ln_beta": np.ascontiguousarray(inputs["L0_attn_ln_beta"], dtype=f32).reshape(32, 128),
        "W1": np.ascontiguousarray(inputs["L0_attn_W1"], dtype=f32),
        "b1": np.ascontiguousarray(inputs["L0_attn_b1"], dtype=f32).reshape(1, H1),
        "W2": np.ascontiguousarray(inputs["L0_attn_W2"], dtype=f32),
        "b2": np.ascontiguousarray(inputs["L0_attn_b2"], dtype=f32).reshape(1, D),
        "clip_x": np.ascontiguousarray(inputs["L0_ffn_clip_x"], dtype=f32).reshape(1, 1),
        "clip_down": np.ascontiguousarray(inputs["L0_ffn_clip_down"], dtype=f32).reshape(1, 1),
        "w_gate": np.ascontiguousarray(inputs["L0_ffn_w_gate"], dtype=f32),
        "w_up": np.ascontiguousarray(inputs["L0_ffn_w_up"], dtype=f32),
        "w_down": np.ascontiguousarray(inputs["L0_ffn_w_down"], dtype=f32),
        **c,
    }
    in_maps = []
    for core in range(NCORE):
        m = dict(base)
        m["x"] = np.ascontiguousarray(x[2 * core:2 * core + 2].reshape(NT * 128, D))
        in_maps.append(m)
    return in_maps


def kernel(**inputs):
    in_maps = make_in_maps(inputs)
    nc = build_program(taps=tuple(DEBUG_TAPS))
    res = run_bass_kernel_spmd(nc, in_maps, core_ids=list(range(NCORE)))
    out = np.empty((16, 2048, D), dtype=np.float32)
    for core in range(NCORE):
        out[2 * core:2 * core + 2] = np.asarray(
            res.results[core]["y"], dtype=np.float32).reshape(2, 2048, D)
    kernel._last_results = res
    return out


# version 03
# speedup vs baseline: 1.0257x; 1.0257x over previous
import numpy as np
import ml_dtypes
from contextlib import ExitStack

import concourse.bass as bass
import concourse.mybir as mybir
from concourse.bass_utils import run_bass_kernel_spmd

F32 = mybir.dt.float32
BF16 = mybir.dt.bfloat16
I32 = mybir.dt.int32
AF = mybir.ActivationFunctionType
ALU = mybir.AluOpType
AX = mybir.AxisListType.X

PI = float(np.pi)
TWO_PI = float(2.0 * np.pi)
HALF_PI = float(np.pi / 2.0)
LN_EPS = 1e-05
D = 1024
CTX = 4096
H1 = 2048
IFF = 2816
NT = 32
TSEQ = 16
NCORE = 8
ENGS = ("sync", "scalar", "vector", "tensor", "gpsimd")

DEBUG_TAPS = []
RACE_DETECT = False
P0_LIMIT = 99
DEBUG_SPILLS = False
SKIP_CASTS = False
SKIP_SET = set()
FULL_SERIAL = False


def _ts(i, w=128):
    return slice(i * w, (i + 1) * w)


class Em:

    def __init__(self, nc):
        self.nc = nc
        self.q = {e: [] for e in ENGS}
        self.sems = {}
        self.cnt = {}
        self.hw = {}
        self.last_on = {}
        self.g = None

    def mk(self, name):
        if name not in self.sems:
            self.sems[name] = self.g.enter_context(self.nc.semaphore(name))
            self.cnt[name] = 0
        return name

    def wait(self, eng, ev):
        if ev is None:
            return
        s, v, peng, is_async = ev
        if v <= 0 or (peng == eng == "tensor" and not is_async):
            return
        key = (eng, s)
        if self.hw.get(key, 0) >= v:
            return
        self.hw[key] = v
        h = self.sems[s]
        self.q[eng].append(lambda e, h=h, v=v: e.wait_ge(h, v))

    def op(self, eng, fn, deps=(), sem=None, inc=0, async_ev=False):
        if FULL_SERIAL:
            deps = list(deps) + ([self.last_ev] if getattr(self, "last_ev", None) else [])
            inc = max(inc, 1)
        if eng in ("vector", "scalar", "gpsimd") and sem is None:
            inc = max(inc, 1)
            prev = self.last_on.get(eng)
            if prev is not None:
                deps = list(deps) + [prev]
        for d in deps:
            self.wait(eng, d)
        if inc:
            s = self.mk(sem if sem is not None else ("s_" + eng))
            self.cnt[s] = self.cnt.get(s, 0) + inc
            v = self.cnt[s]
            h = self.sems[s]
            self.q[eng].append(lambda e, fn=fn, h=h, inc=inc: fn(e).then_inc(h, inc))
            ev = (s, v, eng, True if FULL_SERIAL else async_ev)
            if FULL_SERIAL:
                self.last_ev = ev
            if eng in ("vector", "scalar", "gpsimd"):
                self.last_on[eng] = (s, v, eng, True)
            return ev
        self.q[eng].append(lambda e, fn=fn: fn(e))
        return None

    def dma(self, eng, out, in_, deps=(), slot=None):
        s = self.mk("d_" + (slot or eng))
        if FULL_SERIAL:
            deps = list(deps) + ([self.last_ev] if getattr(self, "last_ev", None) else [])
        for d in deps:
            self.wait(eng, d)
        self.cnt[s] = self.cnt.get(s, 0) + 16
        v = self.cnt[s]
        h = self.sems[s]
        self.q[eng].append(
            lambda e, out=out, in_=in_, h=h: e.dma_start(out=out, in_=in_).then_inc(h, 16))
        ev = (s, v, eng, True)
        if FULL_SERIAL:
            self.last_ev = ev
        return ev

    def dmat(self, eng, out, in_, deps=(), slot=None):
        s = self.mk("d_" + (slot or eng))
        if FULL_SERIAL:
            deps = list(deps) + ([self.last_ev] if getattr(self, "last_ev", None) else [])
        for d in deps:
            self.wait(eng, d)
        self.cnt[s] = self.cnt.get(s, 0) + 16
        v = self.cnt[s]
        h = self.sems[s]
        self.q[eng].append(
            lambda e, out=out, in_=in_, h=h: e.dma_start_transpose(out, in_).then_inc(h, 16))
        ev = (s, v, eng, True)
        if FULL_SERIAL:
            self.last_ev = ev
        return ev

    def barrier(self):
        for eng in ENGS:
            for s, v in list(self.cnt.items()):
                self.wait(eng, (s, v, "__b__", True))

    def flush(self, block):
        for eng in ENGS:
            ops = self.q[eng]
            if not ops:
                continue
            self.q[eng] = []
            getattr(block, eng)(lambda e, ops=ops: [f(e) for f in ops])


def build_program(taps=(), stop_after=None):
    nc = bass.Bass("TRN2", target_bir_lowering=False, debug=False,
                   detect_race_conditions=RACE_DETECT)
    g = ExitStack()
    S = g.enter_context

    d_x = nc.dram_tensor("x", [NT * 128, D], F32, kind="ExternalInput").ap()
    d_Wom = nc.dram_tensor("W_omega", [D, D], F32, kind="ExternalInput").ap()
    d_bom = nc.dram_tensor("b_omega", [1, D], F32, kind="ExternalInput").ap()
    d_ls = nc.dram_tensor("log_scale", [1, D], F32, kind="ExternalInput").ap()
    d_gam = nc.dram_tensor("ln_gamma", [1, CTX], F32, kind="ExternalInput").ap()
    d_beta = nc.dram_tensor("ln_beta", [32, 128], F32, kind="ExternalInput").ap()
    d_W1 = nc.dram_tensor("W1", [H1, CTX], F32, kind="ExternalInput").ap()
    d_b1 = nc.dram_tensor("b1", [1, H1], F32, kind="ExternalInput").ap()
    d_W2 = nc.dram_tensor("W2", [D, H1], F32, kind="ExternalInput").ap()
    d_b2 = nc.dram_tensor("b2", [1, D], F32, kind="ExternalInput").ap()
    d_cx = nc.dram_tensor("clip_x", [1, 1], F32, kind="ExternalInput").ap()
    d_cd = nc.dram_tensor("clip_down", [1, 1], F32, kind="ExternalInput").ap()
    d_wg = nc.dram_tensor("w_gate", [IFF, D], F32, kind="ExternalInput").ap()
    d_wu = nc.dram_tensor("w_up", [IFF, D], F32, kind="ExternalInput").ap()
    d_wd = nc.dram_tensor("w_down", [D, IFF], F32, kind="ExternalInput").ap()
    d_idb = nc.dram_tensor("idb", [128, 128], BF16, kind="ExternalInput").ap()
    d_idf = nc.dram_tensor("idf", [128, 128], F32, kind="ExternalInput").ap()
    d_ltri = nc.dram_tensor("ltri", [128, 128], BF16, kind="ExternalInput").ap()
    d_onesb = nc.dram_tensor("onesb", [1, 128], BF16, kind="ExternalInput").ap()
    d_onesf = nc.dram_tensor("onesf", [1, 128], F32, kind="ExternalInput").ap()
    d_cmw = nc.dram_tensor("cmw", [1, TSEQ * 128], BF16, kind="ExternalInput").ap()
    d_posc = nc.dram_tensor("posc", [TSEQ, 128], F32, kind="ExternalInput").ap()
    d_invp = nc.dram_tensor("invp", [TSEQ, 128], F32, kind="ExternalInput").ap()
    d_cons = nc.dram_tensor("cons", [128, 4], F32, kind="ExternalInput").ap()

    d_y = nc.dram_tensor("y", [NT * 128, D], F32, kind="ExternalOutput").ap()
    spk = dict(kind="ExternalOutput") if DEBUG_SPILLS else {}
    d_wombf = nc.dram_tensor("wombf", [D, D], BF16).ap()
    d_w1bf = nc.dram_tensor("w1bf", [H1, CTX], BF16).ap()
    d_w2bf = nc.dram_tensor("w2bf", [D, H1], BF16).ap()
    d_qwg = nc.dram_tensor("qwg", [IFF, D], BF16, **spk).ap()
    d_qwu = nc.dram_tensor("qwu", [IFF, D], BF16).ap()
    d_qwd = nc.dram_tensor("qwd", [D, IFF], BF16).ap()
    d_ctxc = nc.dram_tensor("ctxc", [NT * 128, CTX], BF16, **spk).ap()
    d_gla = nc.dram_tensor("gla", [NT * 128, H1 // 2], BF16, **spk).ap()
    d_glb = nc.dram_tensor("glb", [NT * 128, H1 // 2], BF16).ap()
    d_xmid = nc.dram_tensor("xmid", [NT * 128, D], F32, **spk).ap()
    d_t = nc.dram_tensor("tbuf", [NT * 128, IFF], F32, **spk).ap()
    cc1i = nc.dram_tensor("cc1i", [1, 128], F32)
    cc1o = nc.dram_tensor("cc1o", [1, 128], F32)
    cc2i = nc.dram_tensor("cc2i", [1, 128], F32)
    cc2o = nc.dram_tensor("cc2o", [1, 128], F32)

    tap_d = {}
    tapdefs = {
        "x1": (D, F32), "sin": (D, F32), "cos": (D, F32), "phir": (D, F32),
        "memr": (D, F32), "memi": (D, F32), "ctx": (CTX, BF16),
        "ctxc": (CTX, BF16), "h1": (H1 // 2, F32), "gl": (H1 // 2, BF16),
        "xmid": (D, F32), "q": (D, BF16), "t": (IFF, F32), "q2": (IFF, BF16),
    }
    for name in taps:
        if name == "scal":
            tap_d["scal"] = nc.dram_tensor("tap_scal", [1, 16], F32,
                                           kind="ExternalOutput").ap()
        else:
            w, dt = tapdefs[name]
            tap_d[name] = nc.dram_tensor("tap_" + name, [NT * 128, w], dt,
                                         kind="ExternalOutput").ap()

    E = Em(nc)
    E.g = g
    for s in ("s_tensor", "s_vector", "s_scalar", "s_gpsimd", "s_cc"):
        E.sems[s] = S(nc.semaphore(s))
        E.cnt[s] = 0

    IDB = S(nc.sbuf_tensor("IDB", [128, 128], BF16))
    IDF = S(nc.sbuf_tensor("IDF", [128, 128], F32))
    LTRI = S(nc.sbuf_tensor("LTRI", [128, 128], BF16))
    ONESB = S(nc.sbuf_tensor("ONESB", [1, 128], BF16))
    ONESF = S(nc.sbuf_tensor("ONESF", [1, 128], F32))
    POSC = S(nc.sbuf_tensor("POSC", [128, TSEQ], F32))
    INVPC = S(nc.sbuf_tensor("INVPC", [128, TSEQ], F32))
    CONS = S(nc.sbuf_tensor("CONS", [128, 4], F32))
    MUX = S(nc.sbuf_tensor("MUX", [128, NT], F32))
    RSTDX = S(nc.sbuf_tensor("RSTDX", [128, NT], F32))
    VARC = S(nc.sbuf_tensor("VARC", [128, NT], F32))
    RSTDC = S(nc.sbuf_tensor("RSTDC", [128, NT], F32))
    MU2 = S(nc.sbuf_tensor("MU2", [128, NT], F32))
    RSTD2 = S(nc.sbuf_tensor("RSTD2", [128, NT], F32))
    RSTD2S = S(nc.sbuf_tensor("RSTD2S", [128, NT], F32))
    TMPA = S(nc.sbuf_tensor("TMPA", [128, NT], F32))
    TMPB = S(nc.sbuf_tensor("TMPB", [128, NT], F32))
    STT = S(nc.sbuf_tensor("STT", [128, 8], F32))
    AMX = S(nc.sbuf_tensor("AMX", [128, 2], F32))
    AMT = S(nc.sbuf_tensor("AMT", [128, 2], F32))
    SCAL = S(nc.sbuf_tensor("SCAL", [1, 16], F32))
    CCROW = S(nc.sbuf_tensor("CCROW", [1, 128], F32))
    SXCOL = S(nc.sbuf_tensor("SXCOL", [128, 1], F32))
    INVSX = S(nc.sbuf_tensor("INVSX", [128, 1], F32))
    SDCOL = S(nc.sbuf_tensor("SDCOL", [128, 1], F32))
    INVSD = S(nc.sbuf_tensor("INVSD", [128, 1], F32))
    XB = [S(nc.sbuf_tensor(f"XBUF{i}", [128, D], F32)) for i in range(2)]
    XM = [S(nc.sbuf_tensor(f"XMB{i}", [128, D], F32)) for i in range(2)]
    TB = [S(nc.sbuf_tensor(f"TBB{i}", [128, IFF], F32)) for i in range(2)]
    X1 = S(nc.sbuf_tensor("X1", [128, D], F32))
    SCR = S(nc.sbuf_tensor("SCRB", [128, D], F32))
    X2S = S(nc.sbuf_tensor("X2S", [128, D], F32))

    nc.const_aps.register(F32, HALF_PI, CONS[:, 0:1])
    nc.const_aps.register(F32, -1.0, CONS[:, 1:2])
    nc.const_aps.register(F32, -0.5, CONS[:, 2:3])

    x_free = [None, None]
    xm_free = [None, None]
    tb_free = [None, None]

    with g, nc.Block() as block:
        p1 = ExitStack()
        P = p1.enter_context
        BETA_TB = S(nc.sbuf_tensor("BETA_TB", [128, 32], BF16))
        BE = P(nc.sbuf_tensor("BE", [1, D], BF16))
        WTE = P(nc.sbuf_tensor("WTE", [128, 8 * D], BF16))
        GAMREP = P(nc.sbuf_tensor("GAMREP", [128, CTX], BF16))
        CMW = P(nc.sbuf_tensor("CMW", [1, TSEQ * 128], BF16))
        p0 = ExitStack()
        P0 = p0.enter_context
        POSL = P0(nc.sbuf_tensor("POSL", [TSEQ, 128], F32))
        INVL = P0(nc.sbuf_tensor("INVL", [TSEQ, 128], F32))
        BETA_L = P0(nc.sbuf_tensor("BETA_L", [32, 128], F32))
        ROWF = P0(nc.sbuf_tensor("ROWF", [1, 3 * D], F32))
        GROWB = P0(nc.sbuf_tensor("GROWB", [1, CTX], BF16))
        EREP = P0(nc.sbuf_tensor("EREP", [128, D], F32))
        WTES = P0(nc.sbuf_tensor("WTES", [128, 8 * D], BF16))
        p_t16 = P0(nc.psum_tensor("p_t16", [128, 64], F32))
        p_bc = P0(nc.psum_tensor("p_bc", [128, 1024], F32))

        e_ms = None
        if "act" not in SKIP_SET:
            e_ms = E.op("vector", lambda v: v.memset(SCAL[0:1, 0:16], 0.0), inc=1)
        if "cst" not in SKIP_SET:
            for dst, srca in ((IDB, d_idb), (IDF, d_idf), (LTRI, d_ltri),
                              (ONESB, d_onesb), (ONESF, d_onesf), (CMW, d_cmw),
                              (POSL, d_posc), (INVL, d_invp), (CONS, d_cons),
                              (BETA_L, d_beta)):
                E.dma("sync", dst[:], srca, slot="cst")
            E.dma("sync", ROWF[0:1, 0:D], d_bom, slot="cst")
            E.dma("sync", ROWF[0:1, 1024:2048], d_ls, slot="cst")
            E.dma("sync", SCAL[0:1, 8:9], d_cx, slot="cst", deps=[e_ms])
            E.dma("sync", SCAL[0:1, 9:10], d_cd, slot="cst", deps=[e_ms])
        else:
            E.mk("d_cst")
        ALLC = ("d_cst", E.cnt["d_cst"], "__x__", True)
        ev_clx = ALLC
        ev_cld = ALLC

        ev_wombf = ev_w2bf = ev_w1bf = None
        if P0_LIMIT >= 2 and not SKIP_CASTS:
            ev_wombf = E.dma("gpsimd", d_wombf, d_Wom, slot="cw0")
            ev_w2bf = E.dma("gpsimd", d_w2bf, d_W2, slot="cw1")
            ev_w1bf = E.dma("gpsimd", d_w1bf, d_W1, slot="cw2")

        T2 = TSEQ
        if "tp" in SKIP_SET:
            e_posc = e_invp = e_beta = None
        if P0_LIMIT < 3:
            E.barrier(); E.flush(block); p0.close(); p1.close(); return nc
        if "tp" not in SKIP_SET:
            ev1 = E.op("tensor", lambda t: t.transpose(p_t16[:, 0:T2], POSL[:, :],
                                                       IDF[0:T2, 0:T2]),
                       deps=[ALLC], inc=1)
            e_posc = E.op("vector", lambda v: v.tensor_copy(POSC[:], p_t16[:, 0:T2]),
                          deps=[ev1], inc=1)
            ev2 = E.op("tensor", lambda t: t.transpose(p_t16[:, T2:2 * T2], INVL[:, :],
                                                       IDF[0:T2, 0:T2]),
                       deps=[e_posc], inc=1)
            e_invp = E.op("vector", lambda v: v.tensor_copy(INVPC[:], p_t16[:, T2:2 * T2]),
                          deps=[ev2], inc=1)
            ev3 = E.op("tensor", lambda t: t.transpose(p_t16[:, 2 * T2:2 * T2 + 32],
                                                       BETA_L[:, :],
                                                       IDF[0:32, 0:32]),
                       deps=[e_invp], inc=1)
            e_beta = E.op("vector", lambda v: v.tensor_copy(BETA_TB[:],
                                                            p_t16[:, 2 * T2:2 * T2 + 32]),
                          deps=[ev3], inc=1)

        if P0_LIMIT < 4:
            E.barrier(); E.flush(block); p0.close(); p1.close(); return nc
        e_exp = e_sgx = e_sgd = e_bE = None
        if "act" not in SKIP_SET:
            e_exp = E.op("scalar", lambda s: s.activation(ROWF[0:1, 2048:3072],
                                                          ROWF[0:1, 1024:2048], AF.Exp),
                         deps=[ALLC], inc=1)
            e_sgx = E.op("scalar", lambda s: s.activation(SCAL[0:1, 10:11],
                                                          SCAL[0:1, 8:9], AF.Sigmoid),
                         deps=[ev_clx], inc=1)
            e_sgd = E.op("scalar", lambda s: s.activation(SCAL[0:1, 11:12],
                                                          SCAL[0:1, 9:10], AF.Sigmoid),
                         deps=[ev_cld], inc=1)
            e_bE = E.op("vector", lambda v: v.tensor_mul(BE[0:1, :], ROWF[0:1, 0:D],
                                                         ROWF[0:1, 2048:3072]),
                        deps=[e_exp], inc=1)
        if P0_LIMIT < 5:
            E.barrier(); E.flush(block); p0.close(); p1.close(); return nc
        gam_evs = []
        if "gam" not in SKIP_SET:
            e_gb = E.dma("gpsimd", GROWB[0:1, :], d_gam, slot="gml")
            for i in range(8):
                mm = E.op("tensor",
                          lambda t, i=i: t.matmul(p_bc[:, _ts(i % 2, 512)], ONESB[0:1, :],
                                                  GROWB[0:1, i * 512:(i + 1) * 512],
                                                  start=True, stop=True),
                          deps=[e_gb, gam_evs[-1]] if gam_evs else [e_gb], inc=1)
                gam_evs.append(E.op("vector",
                                    lambda v, i=i: v.tensor_copy(
                                        GAMREP[:, i * 512:(i + 1) * 512],
                                        p_bc[:, _ts(i % 2, 512)]),
                                    deps=[mm], inc=1))
        if P0_LIMIT < 6:
            E.barrier(); E.flush(block); p0.close(); p1.close(); return nc
        erep_evs = []
        for i in range(2):
            if "erep" in SKIP_SET or "act" in SKIP_SET:
                break
            mm = E.op("tensor",
                      lambda t, i=i: t.matmul(p_bc[:, _ts(i, 512)], ONESF[0:1, :],
                                              ROWF[0:1, 2048 + i * 512:2048 + (i + 1) * 512],
                                              start=True, stop=True),
                      deps=[e_exp] + gam_evs, inc=1)
            erep_evs.append(E.op("vector",
                                 lambda v, i=i: v.tensor_copy(EREP[:, _ts(i, 512)],
                                                              p_bc[:, _ts(i, 512)]),
                                 deps=[mm], inc=1))
        if P0_LIMIT < 7:
            E.barrier(); E.flush(block); p0.close(); p1.close(); return nc
        wte_evs = []
        for c in range(8):
            if SKIP_CASTS:
                continue
            E.dmat("sync", WTES[:, c * D:(c + 1) * D],
                   d_wombf[0:D, c * 128:(c + 1) * 128], deps=[ev_wombf],
                   slot="wte")
        wte_all = (("d_wte", E.cnt["d_wte"], "__x__", True)
                   if "d_wte" in E.cnt else None)
        for c in range(8):
            wte_evs.append(E.op("vector",
                                lambda v, c=c: v.tensor_mul(WTE[:, c * D:(c + 1) * D],
                                                            WTES[:, c * D:(c + 1) * D],
                                                            EREP[:]),
                                deps=[wte_all] + erep_evs, inc=1))

        if stop_after == "P0":
            E.barrier()
            E.flush(block)
            p0.close()
            p1.close()
            return nc

        prev_scr = None
        for j in range(NT):
            b = j % 2
            ld = E.dma("sync", XB[b][:], d_x[_ts(j), :],
                       deps=(x_free[b] if isinstance(x_free[b], list)
                             else [x_free[b]]) if x_free[b] else (), slot=f"x{b}")
            e1 = E.op("scalar", lambda s, b=b: s.activation(
                SCR[:], XB[b][:], AF.Copy, accum_out=STT[:, 0:1]),
                deps=[ld], inc=1)
            e1b = E.op("vector", lambda v, j=j: v.tensor_scalar(
                MUX[:, j:j + 1], STT[:, 0:1], 1.0 / D, None, ALU.mult),
                deps=[e1], inc=1)
            e2 = E.op("scalar", lambda s, b=b: s.activation(
                SCR[:], XB[b][:], AF.Square, accum_out=STT[:, 1:2]),
                deps=[e1b], inc=1)
            x_free[b] = [e2]
            e3 = E.op("vector", lambda v, j=j: v.tensor_mul(
                STT[:, 2:3], MUX[:, j:j + 1], MUX[:, j:j + 1]), deps=[e2], inc=1)
            e4 = E.op("vector", lambda v: v.tensor_scalar(
                STT[:, 3:4], STT[:, 1:2], 1.0 / D, None, ALU.mult), deps=[e2], inc=1)
            e5 = E.op("vector", lambda v, j=j: v.tensor_sub(
                VARC[:, j:j + 1], STT[:, 3:4], STT[:, 2:3]), deps=[e3, e4], inc=1)

        def rstd_batch(VAR_IN, DST, dep):
            ev = E.op("vector", lambda v: v.tensor_scalar(
                TMPA[:, 0:NT], VAR_IN[:, 0:NT], LN_EPS, None, ALU.add),
                deps=[dep], inc=1)
            ev = E.op("scalar", lambda s: s.activation(DST[:, 0:NT], TMPA[:, 0:NT],
                                                       AF.Ln), deps=[ev], inc=1)
            ev = E.op("scalar", lambda s: s.activation(DST[:, 0:NT], DST[:, 0:NT],
                                                       AF.Exp, scale=-0.5),
                      deps=[ev], inc=1)
            ev = E.op("vector", lambda v: v.tensor_mul(TMPB[:, 0:NT], DST[:, 0:NT],
                                                       DST[:, 0:NT]), deps=[ev], inc=1)
            ev = E.op("vector", lambda v: v.tensor_mul(TMPB[:, 0:NT], TMPB[:, 0:NT],
                                                       TMPA[:, 0:NT]), deps=[ev], inc=1)
            ev = E.op("vector", lambda v: v.tensor_scalar(
                TMPB[:, 0:NT], TMPB[:, 0:NT], -0.5, 1.5, ALU.mult, ALU.add),
                deps=[ev], inc=1)
            return E.op("vector", lambda v: v.tensor_mul(DST[:, 0:NT], DST[:, 0:NT],
                                                         TMPB[:, 0:NT]), deps=[ev], inc=1)

        e_rstdx = e5 if "rstd" in SKIP_SET else rstd_batch(VARC, RSTDX, e5)

        if stop_after == "A0":
            E.barrier()
            if DEBUG_SPILLS:
                d_dbg3 = nc.dram_tensor("dbg_stats", [128, 4 * NT], F32,
                                        kind="ExternalOutput").ap()
                E.dma("sync", d_dbg3[:, 0:NT], MUX[:], slot="dbg")
                E.dma("sync", d_dbg3[:, NT:2 * NT], RSTDX[:], slot="dbg")
                E.dma("sync", d_dbg3[:, 2 * NT:3 * NT], TMPA[:], slot="dbg")
                E.dma("sync", d_dbg3[:, 3 * NT:4 * NT], TMPB[:], slot="dbg")
                d_dbg4 = nc.dram_tensor("dbg_xb", [256, D], F32,
                                        kind="ExternalOutput").ap()
                E.dma("sync", d_dbg4[0:128, :], XB[0][:], slot="dbg")
                E.dma("sync", d_dbg4[128:256, :], XB[1][:], slot="dbg")
                E.op("sync", lambda e: e.wait_ge(E.sems["d_dbg"], E.cnt["d_dbg"]))
            E.barrier()
            E.flush(block)
            p0.close()
            p1.close()
            return nc

        E.flush(block)
        p0.close()

        X1T = P(nc.sbuf_tensor("X1T", [128, D], BF16))
        OMS = P(nc.sbuf_tensor("OMS", [128, D], BF16))
        PHR = P(nc.sbuf_tensor("PHR", [128, D], F32))
        RRO = P(nc.sbuf_tensor("RRO", [128, D], F32))
        RRZ = P(nc.sbuf_tensor("RRZ", [128, D], F32))
        SIN = P(nc.sbuf_tensor("SIN", [128, D], F32))
        COSB = P(nc.sbuf_tensor("COSB", [128, D], F32))
        ABSB = P(nc.sbuf_tensor("ABSB", [128, D], F32))
        MEMR = P(nc.sbuf_tensor("MEMR", [128, D], F32))
        MEMI = P(nc.sbuf_tensor("MEMI", [128, D], F32))
        RT1 = P(nc.sbuf_tensor("RT1", [128, D], F32))
        RT2 = P(nc.sbuf_tensor("RT2", [128, D], F32))
        CTXB = P(nc.sbuf_tensor("CTXB", [128, CTX], BF16))
        CTXC = P(nc.sbuf_tensor("CTXC", [128, CTX], BF16))
        ROWB = P(nc.sbuf_tensor("ROWB", [1, 3 * D], BF16))
        p_om = P(nc.psum_tensor("p_om", [128, 1024], F32))
        p_phi = P(nc.psum_tensor("p_phi", [128, 1024], F32))
        p_cm = P(nc.psum_tensor("p_cm", [128, 2048], F32))

        pv = {"x1t": None, "oms": None, "phr": None, "memr": None, "memi": None,
              "rowphi": None, "rowmr": None, "rowmi": None, "sq": None,
              "ctxdma": None}
        tile_taps = []
        for j in range(NT):
            jq = j % TSEQ
            b = j % 2
            ld = E.dma("sync", XB[b][:], d_x[_ts(j), :],
                       deps=((x_free[b] if isinstance(x_free[b], list)
                              else [x_free[b]]) if x_free[b] else []) + tile_taps,
                       slot=f"x{b}")
            tile_taps = []
            e_x1 = E.op("vector", lambda v, b=b, j=j: v.tensor_scalar(
                X1[:], XB[b][:], MUX[:, j:j + 1], RSTDX[:, j:j + 1],
                ALU.subtract, ALU.mult), deps=[ld, e_rstdx], inc=1)
            x_free[b] = e_x1
            tev = None
            for c in range(8):
                tev = E.op("tensor", lambda t, c=c: t.transpose(
                    p_phi[:, _ts(c)], X1[:, _ts(c)], IDF[:]),
                    deps=[e_x1, pv["phr"]] if c == 0 else [],
                    inc=(1 if c == 7 else 0))
            e_x1t = E.op("vector", lambda v: v.tensor_copy(X1T[:], p_phi[:]),
                         deps=[tev], inc=1)
            pv["x1t"] = e_x1t
            first_deps = [e_x1t] + ([pv["oms"]] if pv["oms"] else []) \
                + (wte_evs if j == 0 else [])
            for n in range(2):
                for c in range(8):
                    E.op("tensor", lambda t, c=c, n=n: t.matmul(
                        p_om[:, _ts(n, 512)], X1T[:, _ts(c)],
                        WTE[:, c * D + n * 512: c * D + (n + 1) * 512],
                        start=(c == 0), stop=False),
                        deps=first_deps if (n == 0 and c == 0) else [])
            e_om = None
            for n in range(2):
                e_om = E.op("tensor", lambda t, n=n: t.matmul(
                    p_om[:, _ts(n, 512)], ONESB[0:1, :],
                    BE[0:1, n * 512:(n + 1) * 512],
                    start=False, stop=True),
                    deps=[e_bE] if (j == 0 and n == 0) else [],
                    inc=(1 if n == 1 else 0))
            e_oms = E.op("scalar", lambda s, jq=jq: s.activation(
                OMS[:], p_om[:], AF.Copy, scale=POSC[:, jq:jq + 1]),
                deps=[e_om, e_posc], inc=1)
            pv["oms"] = e_oms
            e_phi = None
            for n in range(2):
                e_phi = E.op("tensor", lambda t, n=n, jq=jq: t.matmul(
                    p_phi[:, _ts(n, 512)], LTRI[:], OMS[:, _ts(n, 512)],
                    start=True, stop=(jq == 0)),
                    deps=[e_oms, e_x1t] if n == 0 else [],
                    inc=(1 if (jq == 0 and n == 1) else 0))
            if jq > 0:
                for n in range(2):
                    e_phi = E.op("tensor", lambda t, n=n: t.matmul(
                        p_phi[:, _ts(n, 512)], ONESB[0:1, :],
                        ROWB[0:1, n * 512:(n + 1) * 512],
                        start=False, stop=True),
                        deps=[pv["rowphi"]] if n == 0 else [],
                        inc=(1 if n == 1 else 0))
            e_o1 = E.op("vector", lambda v: v.tensor_scalar(
                RRO[:], p_phi[:], PI, -TWO_PI, ALU.is_gt, ALU.mult),
                deps=[e_phi], inc=1)
            e_z1 = E.op("vector", lambda v: v.tensor_add(RRZ[:], p_phi[:], RRO[:]),
                        deps=[e_o1], inc=1)
            e_o2 = E.op("vector", lambda v: v.tensor_scalar(
                RRO[:], p_phi[:], -PI, TWO_PI, ALU.is_lt, ALU.mult),
                deps=[e_z1], inc=1)
            e_phr = E.op("vector", lambda v: v.tensor_add(PHR[:], RRZ[:], RRO[:]),
                         deps=[e_o2], inc=1)
            pv["phr"] = e_phr
            if jq != TSEQ - 1:
                pv["rowphi"] = E.dma("gpsimd", ROWB[0:1, 0:D], PHR[127:128, :],
                                     deps=[e_phr], slot="rp")
            e_sin = E.op("scalar", lambda s: s.activation(SIN[:], PHR[:], AF.Sin),
                         deps=[e_phr], inc=1)
            e_abs = E.op("scalar", lambda s: s.activation(ABSB[:], PHR[:], AF.Abs),
                         deps=[e_phr], inc=1)
            e_cos = E.op("scalar", lambda s: s.activation(
                COSB[:], ABSB[:], AF.Sin, bias=HALF_PI, scale=-1.0),
                deps=[e_abs], inc=1)
            e_cmr = E.op("vector", lambda v: v.tensor_mul(CTXB[:, 0:D], X1[:], COSB[:]),
                         deps=[e_cos] + ([pv["sq"]] if pv["sq"] else []), inc=1)
            e_cmi = E.op("vector", lambda v: v.tensor_mul(CTXB[:, D:2 * D], X1[:], SIN[:]),
                         deps=[e_sin], inc=1)
            cm_first = [e_cmr, e_cmi] + ([pv["memr"], pv["memi"]] if pv["memr"] else [])
            e_cm = None
            for (off, rowseg, rdep) in ((0, 1, "rowmr"), (1024, 2, "rowmi")):
                for n in range(2):
                    e_cm = E.op("tensor", lambda t, off=off, n=n, jq=jq: t.matmul(
                        p_cm[:, off + n * 512: off + (n + 1) * 512], LTRI[:],
                        CTXB[:, off + n * 512: off + (n + 1) * 512],
                        start=True, stop=(jq == 0)),
                        deps=cm_first if (off == 0 and n == 0) else [],
                        inc=(1 if (jq == 0 and off == 1024 and n == 1) else 0))
            if jq > 0:
                for (off, rowseg, rdep) in ((0, 1, "rowmr"), (1024, 2, "rowmi")):
                    for n in range(2):
                        e_cm = E.op("tensor", lambda t, off=off, rowseg=rowseg, n=n, jq=jq: t.matmul(
                            p_cm[:, off + n * 512: off + (n + 1) * 512],
                            CMW[0:1, jq * 128:(jq + 1) * 128],
                            ROWB[0:1, rowseg * 1024 + n * 512: rowseg * 1024 + (n + 1) * 512],
                            start=False, stop=True),
                            deps=[pv[rdep]] if n == 0 else [],
                            inc=(1 if (off == 1024 and n == 1) else 0))
            e_memr = E.op("scalar", lambda s, jq=jq: s.activation(
                MEMR[:], p_cm[:, 0:1024], AF.Copy, scale=INVPC[:, jq:jq + 1]),
                deps=[e_cm, e_invp], inc=1)
            e_memi = E.op("scalar", lambda s, jq=jq: s.activation(
                MEMI[:], p_cm[:, 1024:2048], AF.Copy, scale=INVPC[:, jq:jq + 1]),
                deps=[], inc=1)
            pv["memr"], pv["memi"] = e_memr, e_memi
            if jq != TSEQ - 1:
                pv["rowmr"] = E.dma("gpsimd", ROWB[0:1, 1024:2048], MEMR[127:128, :],
                                    deps=[e_memr], slot="rr")
                pv["rowmi"] = E.dma("gpsimd", ROWB[0:1, 2048:3072], MEMI[127:128, :],
                                    deps=[e_memi], slot="ri")
            e_r1 = E.op("vector", lambda v: v.tensor_mul(RT1[:], MEMR[:], COSB[:]),
                        deps=[e_memr], inc=1)
            e_r2 = E.op("vector", lambda v: v.tensor_mul(RT2[:], MEMI[:], SIN[:]),
                        deps=[e_memi], inc=1)
            e_rr = E.op("vector", lambda v: v.tensor_add(CTXB[:, 2 * D:3 * D],
                                                         RT1[:], RT2[:]),
                        deps=[e_r1, e_r2], inc=1)
            e_r3 = E.op("vector", lambda v: v.tensor_mul(RT1[:], MEMI[:], COSB[:]),
                        deps=[e_rr], inc=1)
            e_r4 = E.op("vector", lambda v: v.tensor_mul(RT2[:], MEMR[:], SIN[:]),
                        deps=[e_rr], inc=1)
            e_ri = E.op("vector", lambda v: v.tensor_sub(CTXB[:, 3 * D:4 * D],
                                                         RT1[:], RT2[:]),
                        deps=[e_r3, e_r4], inc=1)
            e_sq = E.op("scalar", lambda s: s.activation(
                CTXC[:], CTXB[:], AF.Square, accum_out=STT[:, 4:5]),
                deps=[e_ri] + ([pv["ctxdma"]] if pv["ctxdma"] else []), inc=1)
            e_ssq = e_sq
            pv["sq"] = e_sq
            e_sum = E.op("vector", lambda v: v.reduce_sum(STT[:, 5:6], CTXB[:], axis=AX),
                         deps=[e_ri], inc=1)
            e_mu = E.op("vector", lambda v: v.tensor_scalar(
                STT[:, 6:7], STT[:, 5:6], 1.0 / CTX, None, ALU.mult),
                deps=[e_sum], inc=1)
            e_m2 = E.op("vector", lambda v: v.tensor_mul(STT[:, 7:8], STT[:, 6:7],
                                                         STT[:, 6:7]),
                        deps=[e_mu], inc=1)
            e_vv = E.op("vector", lambda v, j=j: v.tensor_scalar(
                TMPA[:, j:j + 1], STT[:, 4:5], 1.0 / CTX, None, ALU.mult),
                deps=[e_ssq, e_m2], inc=1)
            e_var = E.op("vector", lambda v, j=j: v.tensor_sub(
                TMPA[:, j:j + 1], TMPA[:, j:j + 1], STT[:, 7:8]), deps=[e_vv], inc=1)
            e_ctxc = E.op("vector", lambda v: v.scalar_tensor_tensor(
                CTXC[:], CTXB[:], STT[:, 6:7], GAMREP[:], ALU.subtract, ALU.mult),
                deps=[e_ssq, e_mu] + (gam_evs if j == 0 else []), inc=1)
            pv["ctxdma"] = E.dma("scalar", d_ctxc[_ts(j), :], CTXC[:],
                                 deps=[e_ctxc], slot="sc")
            if "x1" in tap_d:
                tile_taps.append(E.dma("gpsimd", tap_d["x1"][_ts(j), :], X1[:], deps=[e_x1, e_cmi, e_cmr], slot="tap"))
            if "sin" in tap_d:
                tile_taps.append(E.dma("gpsimd", tap_d["sin"][_ts(j), :], SIN[:], deps=[e_sin, e_cmi, e_r2, e_r4], slot="tap"))
            if "cos" in tap_d:
                tile_taps.append(E.dma("gpsimd", tap_d["cos"][_ts(j), :], COSB[:], deps=[e_cos, e_cmr, e_r1, e_r3], slot="tap"))
            if "phir" in tap_d:
                tile_taps.append(E.dma("gpsimd", tap_d["phir"][_ts(j), :], PHR[:], deps=[e_phr, e_sin, e_abs], slot="tap"))
            if "memr" in tap_d:
                tile_taps.append(E.dma("gpsimd", tap_d["memr"][_ts(j), :], MEMR[:], deps=[e_r1, e_r4], slot="tap"))
            if "memi" in tap_d:
                tile_taps.append(E.dma("gpsimd", tap_d["memi"][_ts(j), :], MEMI[:], deps=[e_r2, e_r3], slot="tap"))
            if "ctx" in tap_d:
                tile_taps.append(E.dma("gpsimd", tap_d["ctx"][_ts(j), :], CTXB[:], deps=[e_ri, e_sq], slot="tap"))
            if "ctxc" in tap_d:
                tile_taps.append(E.dma("gpsimd", tap_d["ctxc"][_ts(j), :], CTXC[:], deps=[e_ctxc], slot="tap"))
        e_varc_last = e_var

        e_varc = E.op("vector", lambda v: v.tensor_copy(VARC[:, 0:NT], TMPA[:, 0:NT]),
                      deps=[e_varc_last, e_rstdx], inc=1)

        if stop_after == "A1" and DEBUG_SPILLS:
            E.barrier()
            d_dbg1 = nc.dram_tensor("dbg_gam", [128, CTX], BF16,
                                    kind="ExternalOutput").ap()
            d_dbg2 = nc.dram_tensor("dbg_wte", [128, 8 * D], BF16,
                                    kind="ExternalOutput").ap()
            d_dbg3 = nc.dram_tensor("dbg_stats", [128, 4 * NT], F32,
                                    kind="ExternalOutput").ap()
            E.dma("sync", d_dbg1, GAMREP[:], deps=[], slot="dbg")
            E.dma("sync", d_dbg2, WTE[:], deps=[], slot="dbg")
            E.dma("sync", d_dbg3[:, 0:NT], MUX[:], slot="dbg")
            E.dma("sync", d_dbg3[:, NT:2 * NT], RSTDX[:], slot="dbg")
            E.dma("sync", d_dbg3[:, 2 * NT:3 * NT], TMPA[:], slot="dbg")
            E.dma("sync", d_dbg3[:, 3 * NT:4 * NT], VARC[:], slot="dbg")
            E.op("sync", lambda e: e.wait_ge(E.sems["d_dbg"], E.cnt["d_dbg"]))
        E.barrier()
        E.flush(block)
        p1.close()
        if stop_after == "A1":
            return nc

        p2 = ExitStack()
        P = p2.enter_context
        W1TH = P(nc.sbuf_tensor("W1TH", [128, 32 * 1024], BF16))
        CLD = [P(nc.sbuf_tensor(f"CLD{i}", [128, CTX], BF16)) for i in range(2)]
        CLT = P(nc.sbuf_tensor("CLT", [128, CTX], BF16))
        H1B = P(nc.sbuf_tensor("H1B", [128, 1024], F32))
        GLO = [P(nc.sbuf_tensor(f"GLO{i}", [128, 1024], BF16)) for i in range(2)]
        B1REP = P(nc.sbuf_tensor("B1REP", [128, H1], F32))
        B1ROW = P(nc.sbuf_tensor("B1ROW", [1, H1 + 1024], F32))
        WQS = P(nc.sbuf_tensor("WQS", [128, 1024], F32))
        WQI = P(nc.sbuf_tensor("WQI", [128, 1024], I32))
        WQO = P(nc.sbuf_tensor("WQO", [128, 1024], BF16))
        p_ct = P(nc.psum_tensor("p_ct", [128, 2048], BF16))
        p_h1 = P(nc.psum_tensor("p_h1", [128, 1024], F32))
        p_b1 = P(nc.psum_tensor("p_b1", [1, 1024], F32))
        p_bc2 = P(nc.psum_tensor("p_bc2", [128, 1024], F32))

        e_rstdc = rstd_batch(VARC, RSTDC, e_varc)
        e_b1ld = E.dma("sync", B1ROW[0:1, 0:H1], d_b1, slot="b1l")

        qchunks = []
        for (src, dst, nch, w) in ((d_wg, d_qwg, 22, D), (d_wu, d_qwu, 22, D)):
            for c in range(nch):
                qchunks.append((src, dst, c, w))
        for c in range(8):
            qchunks.append((d_wd, d_qwd, c, IFF))
        qspill = {"last_g": None, "last_u": None, "last_d": None}
        qprev = [None, None]

        def emit_quant_chunk(idx):
            src, dst, c, w = qchunks[idx]
            if w == D:
                stage = WQS
                ld = E.dma("sync", stage[:, 0:w], src[_ts(c), :],
                           deps=[qprev[0]] if qprev[0] else (), slot="qs")
            else:
                stage = TB[0]
                ld = E.dma("sync", stage[:, 0:w], src[_ts(c, 128), :],
                           deps=[qprev[0]] if qprev[0] else (), slot="qs")
            e_am = E.op("vector", lambda v, stage=stage, w=w: v.reduce_max(
                STT[:, 0:1], stage[:, 0:w], axis=AX, apply_absolute_value=True),
                deps=[ld], inc=1)
            e_s = E.op("vector", lambda v: v.tensor_scalar(
                STT[:, 1:2], STT[:, 0:1], 1.0 / 127.0, None, ALU.mult),
                deps=[e_am], inc=1)
            e_r = E.op("vector", lambda v: v.reciprocal(STT[:, 2:3], STT[:, 1:2]),
                       deps=[e_s], inc=1)
            spills = []
            nsub = (w + 1023) // 1024
            for sub in range(nsub):
                lo = sub * 1024
                hi = min(w, lo + 1024)
                sw = hi - lo
                e_t = E.op("vector", lambda v, stage=stage, lo=lo, sw=sw: v.tensor_scalar(
                    WQI[:, 0:sw],
                    stage[:, lo:lo + sw], STT[:, 2:3], None, ALU.mult),
                    deps=[e_r] + ([qprev[0]] if qprev[0] else []) +
                         ([qprev[1]] if qprev[1] else []), inc=1)
                e_qi = E.op("vector", lambda v, sw=sw: v.tensor_scalar(
                    WQI[:, 0:sw], WQI[:, 0:sw], 127.0, -128.0, ALU.min, ALU.max),
                    deps=[e_t], inc=1)
                e_qo = E.op("vector", lambda v, sw=sw: v.tensor_scalar(
                    WQO[:, 0:sw], WQI[:, 0:sw], STT[:, 1:2], None, ALU.mult),
                    deps=[e_qi] + ([qprev[0]] if qprev[0] else []), inc=1)
                qprev[1] = e_qo
                if w == D:
                    sp = E.dma("gpsimd", dst[_ts(c), :], WQO[:, 0:sw], deps=[e_qo],
                               slot="spq")
                else:
                    sp = E.dma("gpsimd", dst[_ts(c, 128), lo:lo + sw], WQO[:, 0:sw],
                               deps=[e_qo], slot="spq")
                spills.append(sp)
                qprev[0] = sp
            if dst is d_qwg and c == 21:
                qspill["last_g"] = spills[-1]
            if dst is d_qwu and c == 21:
                qspill["last_u"] = spills[-1]
            if dst is d_qwd and c == 7:
                qspill["last_d"] = spills[-1]

        gl_dsts = (d_gla, d_glb)
        a2taps = []
        prev_w1_consumed = []
        cld_free = [None, None]
        glo_free = [None, None]
        prev_ev = {0: None, 1: None}
        prev_te = {0: None, 1: None}
        prev_gelu = None
        last_gl_dma = None
        qemit = 0
        for half in range(2):
            w1_lds = []
            for k in range(32):
                w1_lds.append(E.dmat(
                    "scalar", W1TH[:, k * 1024:(k + 1) * 1024],
                    d_w1bf[half * 1024:(half + 1) * 1024, k * 128:(k + 1) * 128],
                    deps=([ev_w1bf] if half == 0 and k == 0 else []) +
                         (list(prev_w1_consumed) if k == 0 else []),
                    slot="w1l"))
            b1mm = None
            for k in range(32):
                for n in range(2):
                    b1mm = E.op("tensor", lambda t, k=k, n=n: t.matmul(
                        p_b1[0:1, n * 512:(n + 1) * 512], BETA_TB[:, k:k + 1],
                        W1TH[:, k * 1024 + n * 512: k * 1024 + (n + 1) * 512],
                        start=(k == 0), stop=(k == 31)),
                        deps=[w1_lds[-1], e_beta] if (k == 0 and n == 0) else [],
                        inc=(1 if (k == 31 and n == 1) else 0))
            e_b1r = E.op("vector", lambda v, half=half: v.tensor_add(
                B1ROW[0:1, 2048:3072], p_b1[0:1, 0:1024],
                B1ROW[0:1, half * 1024:(half + 1) * 1024]),
                deps=[b1mm, e_b1ld], inc=1)
            b1revs = []
            for n in range(2):
                mm = E.op("tensor", lambda t, n=n: t.matmul(
                    p_bc2[:, _ts(n, 512)], ONESF[0:1, :],
                    B1ROW[0:1, 2048 + n * 512:2048 + (n + 1) * 512],
                    start=True, stop=True), deps=[e_b1r], inc=1)
                b1revs.append(E.op("vector", lambda v, n=n, half=half: v.tensor_copy(
                    B1REP[:, half * 1024 + n * 512: half * 1024 + (n + 1) * 512],
                    p_bc2[:, _ts(n, 512)]), deps=[mm], inc=1))

            for j in range(NT):
                b = j % 2
                ld = E.dma("sync", CLD[b][:], d_ctxc[_ts(j), :],
                           deps=([cld_free[b]] if cld_free[b] else []) + a2taps,
                           slot=f"cl{b}")
                a2taps = []
                bevs = []
                for bb in range(4):
                    par = bb % 2
                    tev = None
                    for c in range(8):
                        cc = bb * 8 + c
                        tev = E.op("tensor", lambda t, b=b, cc=cc, par=par, c=c: t.transpose(
                            p_ct[:, par * 1024 + c * 128: par * 1024 + (c + 1) * 128],
                            CLD[b][:, _ts(cc)], IDB[:]),
                            deps=[ld, prev_te[par]] if c == 0 else [],
                            inc=(1 if c == 7 else 0))
                    bevs.append(E.op("vector", lambda v, bb=bb, par=par: v.tensor_copy(
                        CLT[:, bb * 1024:(bb + 1) * 1024],
                        p_ct[:, par * 1024:par * 1024 + 1024]),
                        deps=[tev], inc=1))
                    prev_te[par] = bevs[-1]
                cld_free[b] = bevs[-1]
                for n in range(2):
                    lastm = None
                    for k in range(32):
                        lastm = E.op("tensor", lambda t, n=n, k=k: t.matmul(
                            p_h1[:, _ts(n, 512)], CLT[:, _ts(k)],
                            W1TH[:, k * 1024 + n * 512: k * 1024 + (n + 1) * 512],
                            start=(k == 0), stop=(k == 31)),
                            deps=(bevs + [w1_lds[-1]] +
                                  ([prev_ev[n]] if prev_ev[n] else [])) if k == 0 else [],
                            inc=(1 if k == 31 else 0))
                    prev_ev[n] = E.op("vector", lambda v, n=n, j=j, half=half: v.scalar_tensor_tensor(
                        H1B[:, _ts(n, 512)], p_h1[:, _ts(n, 512)], RSTDC[:, j:j + 1],
                        B1REP[:, half * 1024 + n * 512: half * 1024 + (n + 1) * 512],
                        ALU.mult, ALU.add),
                        deps=[lastm, e_rstdc] + ([prev_gelu] if prev_gelu else []) +
                             (b1revs if j == 0 else []), inc=1)
                e_gl = E.op("scalar", lambda s, b=b: s.activation(
                    GLO[b][:], H1B[:], AF.Gelu),
                    deps=[prev_ev[0], prev_ev[1]] +
                         ([glo_free[b]] if glo_free[b] else []), inc=1)
                prev_gelu = e_gl
                last_gl_dma = E.dma("scalar", gl_dsts[half][_ts(j), :], GLO[b][:],
                                    deps=[e_gl], slot=f"sg{b}")
                glo_free[b] = last_gl_dma
                if "h1" in tap_d and half == 0:
                    a2taps.append(E.dma("gpsimd", tap_d["h1"][_ts(j), :], H1B[:],
                          deps=[prev_ev[0], prev_ev[1], e_gl], slot="tap"))
                if "gl" in tap_d and half == 0:
                    a2taps.append(E.dma("gpsimd", tap_d["gl"][_ts(j), :], GLO[b][:],
                                        deps=[e_gl], slot="tap"))
                while qemit < len(qchunks) and qemit < (half * NT + j + 1):
                    emit_quant_chunk(qemit)
                    qemit += 1
            prev_w1_consumed = [prev_ev[0], prev_ev[1]]
        while qemit < len(qchunks):
            emit_quant_chunk(qemit)
            qemit += 1

        E.barrier()
        E.flush(block)
        p2.close()
        if stop_after == "A2a":
            return nc

        p3 = ExitStack()
        P = p3.enter_context
        W2T = P(nc.sbuf_tensor("W2T", [128, 16 * 1024], BF16))
        GLD = [P(nc.sbuf_tensor(f"GLD{i}", [128, H1], BF16)) for i in range(2)]
        GLT = P(nc.sbuf_tensor("GLT", [128, H1], BF16))
        B2B = P(nc.sbuf_tensor("B2B", [1, D + 1024], F32))
        B2BB = P(nc.sbuf_tensor("B2BB", [1, D], BF16))
        p_gt = P(nc.psum_tensor("p_gt", [128, 2048], BF16))
        p_h2 = P(nc.psum_tensor("p_h2", [128, 1024], F32))

        w2_lds = []
        for k in range(16):
            w2_lds.append(E.dmat("sync", W2T[:, k * 1024:(k + 1) * 1024],
                                 d_w2bf[0:D, k * 128:(k + 1) * 128],
                                 deps=[ev_w2bf] if k == 0 else (), slot="w2l"))
        e_b2ld = E.dma("sync", B2B[0:1, 0:D], d_b2, slot="b2l")
        e_b2b = E.op("vector", lambda v: v.tensor_copy(B2BB[0:1, :], B2B[0:1, 0:D]),
                     deps=[e_b2ld], inc=1)
        e_amz = E.op("vector", lambda v: v.memset(AMX[:, 0:2], 0.0), inc=1)
        prev_am = e_amz
        amax_par = 0
        gld_free = [None, None]
        prev_gt = {0: None, 1: None}
        prev_h2 = None
        xm_wfree = [[], []]
        prev_scr2 = None
        xmid_dmas = []
        bttaps = []
        for j in range(NT):
            b = j % 2
            ldg0 = E.dma("sync", GLD[b][:, 0:1024], d_gla[_ts(j), :],
                         deps=([gld_free[b]] if gld_free[b] else []) + bttaps,
                         slot=f"ga{b}")
            bttaps = []
            ldg1 = E.dma("sync", GLD[b][:, 1024:2048], d_glb[_ts(j), :],
                         deps=[gld_free[b]] if gld_free[b] else (), slot=f"gb{b}")
            ldx = E.dma("sync", XB[b][:], d_x[_ts(j), :],
                        deps=(x_free[b] if isinstance(x_free[b], list)
                              else [x_free[b]]) if x_free[b] else (), slot=f"x{b}")
            gevs = []
            for bb in range(2):
                tev = None
                for c in range(8):
                    cc = bb * 8 + c
                    tev = E.op("tensor", lambda t, b=b, cc=cc, bb=bb, c=c: t.transpose(
                        p_gt[:, bb * 1024 + c * 128: bb * 1024 + (c + 1) * 128],
                        GLD[b][:, _ts(cc)], IDB[:]),
                        deps=[ldg0, ldg1, prev_gt[bb]] if c == 0 else [],
                        inc=(1 if c == 7 else 0))
                gevs.append(E.op("vector", lambda v, bb=bb: v.tensor_copy(
                    GLT[:, bb * 1024:(bb + 1) * 1024],
                    p_gt[:, bb * 1024:bb * 1024 + 1024]),
                    deps=[tev], inc=1))
                prev_gt[bb] = gevs[-1]
            gld_free[b] = gevs[-1]
            for n in range(2):
                for k in range(16):
                    E.op("tensor", lambda t, n=n, k=k: t.matmul(
                        p_h2[:, _ts(n, 512)], GLT[:, _ts(k)],
                        W2T[:, k * 1024 + n * 512: k * 1024 + (n + 1) * 512],
                        start=(k == 0), stop=False),
                        deps=(gevs + [w2_lds[-1]] +
                              ([prev_h2] if prev_h2 else [])) if (n == 0 and k == 0) else [])
            lastm = None
            for n in range(2):
                lastm = E.op("tensor", lambda t, n=n: t.matmul(
                    p_h2[:, _ts(n, 512)], ONESB[0:1, :],
                    B2BB[0:1, n * 512:(n + 1) * 512],
                    start=False, stop=True),
                    deps=[e_b2b] if n == 0 else [], inc=(1 if n == 1 else 0))
            e_x1 = E.op("vector", lambda v, b=b, j=j: v.tensor_scalar(
                X1[:], XB[b][:], MUX[:, j:j + 1], RSTDX[:, j:j + 1],
                ALU.subtract, ALU.mult), deps=[ldx], inc=1)
            e_xm1 = E.op("vector", lambda v, b=b: v.tensor_add(X2S[:], XB[b][:], X1[:]),
                         deps=[e_x1] + ([xm_free[b]] if xm_free[b] else []), inc=1)
            x_free[b] = e_xm1
            exm = []
            for n in range(2):
                exm.append(E.op("vector", lambda v, b=b, n=n: v.tensor_add(
                    XM[b][:, _ts(n, 512)], X2S[:, _ts(n, 512)], p_h2[:, _ts(n, 512)]),
                    deps=[lastm] + (xm_wfree[b] if n == 0 else []), inc=1))
            prev_h2 = exm[-1]
            xmid_dmas.append(E.dma("scalar", d_xmid[_ts(j), :], XM[b][:], deps=exm,
                                   slot=f"sx{b}"))
            e1 = E.op("scalar", lambda s, b=b: s.activation(
                SCR[:], XM[b][:], AF.Copy, accum_out=STT[:, 0:1]), deps=exm, inc=1)
            e_mu = E.op("vector", lambda v, j=j: v.tensor_scalar(
                MU2[:, j:j + 1], STT[:, 0:1], 1.0 / D, None, ALU.mult),
                deps=[e1], inc=1)
            e_sq2 = E.op("scalar", lambda s, b=b: s.activation(
                SCR[:], XM[b][:], AF.Square, accum_out=STT[:, 1:2]),
                deps=[e_mu], inc=1)
            e2 = e_sq2
            xm_wfree[b] = [xmid_dmas[-1], e_sq2]
            e3 = E.op("vector", lambda v, j=j: v.tensor_mul(
                STT[:, 2:3], MU2[:, j:j + 1], MU2[:, j:j + 1]), deps=[e2], inc=1)
            e4 = E.op("vector", lambda v: v.tensor_scalar(
                STT[:, 3:4], STT[:, 1:2], 1.0 / D, None, ALU.mult), deps=[e2], inc=1)
            e5 = E.op("vector", lambda v: v.tensor_sub(STT[:, 4:5], STT[:, 3:4],
                                                       STT[:, 2:3]),
                      deps=[e3, e4], inc=1)
            e6 = E.op("vector", lambda v: v.tensor_scalar(
                STT[:, 5:6], STT[:, 4:5], LN_EPS, None, ALU.add), deps=[e5], inc=1)
            e7 = E.op("scalar", lambda s, j=j: s.activation(
                RSTD2[:, j:j + 1], STT[:, 5:6], AF.Ln), deps=[e6], inc=1)
            e8 = E.op("scalar", lambda s, j=j: s.activation(
                RSTD2[:, j:j + 1], RSTD2[:, j:j + 1], AF.Exp, scale=-0.5),
                deps=[e7], inc=1)
            e9 = E.op("vector", lambda v, j=j: v.tensor_mul(
                STT[:, 6:7], RSTD2[:, j:j + 1], RSTD2[:, j:j + 1]), deps=[e8], inc=1)
            e10 = E.op("vector", lambda v: v.tensor_mul(STT[:, 6:7], STT[:, 6:7],
                                                        STT[:, 5:6]), deps=[e9], inc=1)
            e11 = E.op("vector", lambda v: v.tensor_scalar(
                STT[:, 6:7], STT[:, 6:7], -0.5, 1.5, ALU.mult, ALU.add),
                deps=[e10], inc=1)
            e12 = E.op("vector", lambda v, j=j: v.tensor_mul(
                RSTD2[:, j:j + 1], RSTD2[:, j:j + 1], STT[:, 6:7]), deps=[e11], inc=1)
            e_x2 = E.op("vector", lambda v, b=b, j=j: v.tensor_scalar(
                X2S[:], XM[b][:], MU2[:, j:j + 1], RSTD2[:, j:j + 1],
                ALU.subtract, ALU.mult), deps=[e12], inc=1)
            xm_free[b] = e_x2
            e_am = E.op("vector", lambda v: v.reduce_max(
                STT[:, 7:8], X2S[:], axis=AX, apply_absolute_value=True),
                deps=[e_x2], inc=1)
            prev_am = E.op("vector", lambda v, ap=amax_par: v.tensor_max(
                AMX[:, 1 - ap:2 - ap], AMX[:, ap:ap + 1], STT[:, 7:8]),
                deps=[e_am, prev_am], inc=1)
            amax_par = 1 - amax_par
            if "xmid" in tap_d:
                bttaps.append(E.dma("gpsimd", tap_d["xmid"][_ts(j), :], XM[b][:],
                                    deps=[e_x2] + exm, slot="tap"))
        e_rstd2_last = e12

        E.barrier()
        E.flush(block)
        p3.close()
        if stop_after == "A2b":
            return nc

        p4 = ExitStack()
        P = p4.enter_context
        p_sc = P(nc.psum_tensor("p_sc", [128, 128], F32))
        ev = E.op("tensor", lambda t, ap=amax_par: t.transpose(
            p_sc[0:1, 0:128], AMX[:, ap:ap + 1], IDF[:]), deps=[prev_am], inc=1)
        e_row = E.op("vector", lambda v: v.tensor_copy(CCROW[0:1, :], p_sc[0:1, 0:128]),
                     deps=[ev], inc=1)
        e_w = E.dma("gpsimd", cc1i.ap(), CCROW[0:1, :], deps=[e_row], slot="ccd")
        e_cc = E.op("gpsimd", lambda gg: gg.collective_compute(
            "AllReduce", ALU.max, replica_groups=[list(range(NCORE))],
            ins=[cc1i.ap()], outs=[cc1o.ap()]), deps=[e_w], sem="s_cc", inc=1,
            async_ev=True)
        e_r = E.dma("gpsimd", CCROW[0:1, :], cc1o.ap(), deps=[e_cc], slot="ccd")
        e_gm = E.op("vector", lambda v: v.reduce_max(SCAL[0:1, 0:1], CCROW[0:1, :],
                                                     axis=AX), deps=[e_r], inc=1)
        e_sx = E.op("vector", lambda v: v.tensor_mul(SCAL[0:1, 1:2], SCAL[0:1, 0:1],
                                                     SCAL[0:1, 10:11]),
                    deps=[e_gm, e_sgx], inc=1)
        e_sx2 = E.op("vector", lambda v: v.tensor_scalar(
            SCAL[0:1, 1:2], SCAL[0:1, 1:2], 1.0 / 127.0, None, ALU.mult),
            deps=[e_sx], inc=1)
        e_isx = E.op("vector", lambda v: v.reciprocal(SCAL[0:1, 2:3], SCAL[0:1, 1:2]),
                     deps=[e_sx2], inc=1)
        bc1 = []
        prevcp = e_isx
        for (col, dst) in ((1, SXCOL), (2, INVSX)):
            mm = E.op("tensor", lambda t, col=col: t.matmul(
                p_sc[:, 0:1], ONESF[0:1, :], SCAL[0:1, col:col + 1],
                start=True, stop=True), deps=[e_isx, prevcp], inc=1)
            prevcp = E.op("vector", lambda v, dst=dst: v.tensor_copy(
                dst[:], p_sc[:, 0:1]), deps=[mm], inc=1)
            bc1.append(prevcp)
        e_r2s = E.op("vector", lambda v: v.tensor_scalar(
            RSTD2S[:, 0:NT], RSTD2[:, 0:NT], INVSX[:], None, ALU.mult),
            deps=bc1 + [e_rstd2_last], inc=1)

        E.barrier()
        E.flush(block)
        p4.close()
        if stop_after == "CC1":
            return nc

        p5 = ExitStack()
        P = p5.enter_context
        WGT = P(nc.sbuf_tensor("WGT", [128, 8 * IFF], BF16))
        WUT = P(nc.sbuf_tensor("WUT", [128, 8 * IFF], BF16))
        QI = P(nc.sbuf_tensor("QI", [128, D], I32))
        QB = P(nc.sbuf_tensor("QB", [128, D], BF16))
        QT = P(nc.sbuf_tensor("QT", [128, D], BF16))
        SILU = P(nc.sbuf_tensor("SILU", [128, 1024], F32))
        UPS = P(nc.sbuf_tensor("UPS", [128, 1024], F32))
        p_qt = P(nc.psum_tensor("p_qt", [128, 1024], BF16))
        p_g = P(nc.psum_tensor("p_g", [128, 1024], F32))
        p_u = P(nc.psum_tensor("p_u", [128, 1024], F32))

        wg_lds, wu_lds = [], []
        for c in range(8):
            wg_lds.append(E.dmat("sync", WGT[:, c * IFF:(c + 1) * IFF],
                                 d_qwg[0:IFF, c * 128:(c + 1) * 128],
                                 deps=[qspill["last_g"]] if c == 0 else (),
                                 slot="wgl"))
        for c in range(8):
            wu_lds.append(E.dmat("sync", WUT[:, c * IFF:(c + 1) * IFF],
                                 d_qwu[0:IFF, c * 128:(c + 1) * 128],
                                 deps=[qspill["last_u"]] if c == 0 else (),
                                 slot="wul"))
        NCH = [(0, 512), (512, 512), (1024, 512), (1536, 512), (2048, 512),
               (2560, 256)]
        e_amz2 = E.op("vector", lambda v: v.memset(AMT[:, 0:2], 0.0), inc=1)
        prev_amt = e_amz2
        tpar = 0
        prev_qt = None
        prev_qb = None
        prev_sil = {0: None, 1: None}
        prev_ups = {0: None, 1: None}
        prev_tmul = {0: None, 1: None}
        t_dmas = []
        btaps = []
        for j in range(NT):
            b = j % 2
            ldm = E.dma("sync", XM[b][:], d_xmid[_ts(j), :],
                        deps=[xmid_dmas[j]] + ([xm_free[b]] if xm_free[b] else []) +
                             btaps,
                        slot=f"xm{b}")
            btaps = []
            e_z = E.op("vector", lambda v, b=b, j=j: v.tensor_scalar(
                X2S[:], XM[b][:], MU2[:, j:j + 1], RSTD2S[:, j:j + 1],
                ALU.subtract, ALU.mult), deps=[ldm, e_r2s], inc=1)
            xm_free[b] = e_z
            e_qi = E.op("vector", lambda v: v.tensor_scalar(
                QI[:], X2S[:], 127.49, -128.49, ALU.min, ALU.max),
                deps=[e_z] + ([prev_qb] if prev_qb else []), inc=1)
            e_qb = E.op("scalar", lambda s: s.activation(QB[:], QI[:], AF.Copy),
                        deps=[e_qi] + ([prev_qt] if prev_qt else []), inc=1)
            prev_qb = e_qb
            tev = None
            for c in range(8):
                tev = E.op("tensor", lambda t, c=c: t.transpose(
                    p_qt[:, _ts(c)], QB[:, _ts(c)], IDB[:]),
                    deps=[e_qb] + ([prev_qt] if prev_qt else []) if c == 0 else [],
                    inc=(1 if c == 7 else 0))
            e_qt = E.op("vector", lambda v: v.tensor_copy(QT[:], p_qt[:, 0:1024]),
                        deps=[tev], inc=1)
            prev_qt = e_qt
            if "q" in tap_d:
                btaps.append(E.dma("gpsimd", tap_d["q"][_ts(j), :], QB[:],
                                   deps=[e_qb, tev], slot="tap"))
            last_t = None
            for ci, (off, w) in enumerate(NCH):
                par = ci % 2
                lastg = None
                for k in range(8):
                    lastg = E.op("tensor", lambda t, k=k, off=off, w=w, par=par: t.matmul(
                        p_g[:, par * 512:par * 512 + w], QT[:, _ts(k)],
                        WGT[:, k * IFF + off: k * IFF + off + w],
                        start=(k == 0), stop=(k == 7)),
                        deps=([e_qt] + ([prev_sil[par]] if prev_sil[par] else []) +
                              ([wg_lds[-1]] if j == 0 else [])) if k == 0 else [],
                        inc=(1 if k == 7 else 0))
                lastu = None
                for k in range(8):
                    lastu = E.op("tensor", lambda t, k=k, off=off, w=w, par=par: t.matmul(
                        p_u[:, par * 512:par * 512 + w], QT[:, _ts(k)],
                        WUT[:, k * IFF + off: k * IFF + off + w],
                        start=(k == 0), stop=(k == 7)),
                        deps=(([prev_ups[par]] if prev_ups[par] else []) +
                              ([wu_lds[-1]] if j == 0 else [])) if k == 0 else [],
                        inc=(1 if k == 7 else 0))
                e_sil = E.op("scalar", lambda s, w=w, par=par: s.activation(
                    SILU[:, par * 512:par * 512 + w], p_g[:, par * 512:par * 512 + w],
                    AF.Silu, scale=SXCOL[:]),
                    deps=[lastg] + ([prev_tmul[par]] if prev_tmul[par] else []) +
                         (bc1 if j == 0 else []), inc=1)
                prev_sil[par] = e_sil
                e_up = E.op("scalar", lambda s, w=w, par=par: s.activation(
                    UPS[:, par * 512:par * 512 + w], p_u[:, par * 512:par * 512 + w],
                    AF.Copy, scale=SXCOL[:]), deps=[lastu], inc=1)
                prev_ups[par] = e_up
                last_t = E.op("vector", lambda v, b=b, off=off, w=w, par=par: v.tensor_mul(
                    TB[b][:, off:off + w], SILU[:, par * 512:par * 512 + w],
                    UPS[:, par * 512:par * 512 + w]),
                    deps=[e_sil, e_up] +
                         ([tb_free[b]] if (ci == 0 and tb_free[b]) else []), inc=1)
                prev_tmul[par] = last_t
            e_amt = E.op("vector", lambda v, b=b: v.reduce_max(
                STT[:, 7:8], TB[b][:], axis=AX, apply_absolute_value=True),
                deps=[last_t], inc=1)
            prev_amt = E.op("vector", lambda v, tp=tpar: v.tensor_max(
                AMT[:, 1 - tp:2 - tp], AMT[:, tp:tp + 1], STT[:, 7:8]),
                deps=[e_amt, prev_amt], inc=1)
            tpar = 1 - tpar
            t_dmas.append(E.dma("scalar", d_t[_ts(j), :], TB[b][:], deps=[last_t],
                                slot=f"st{b}"))
            tb_free[b] = t_dmas[-1]
            if "t" in tap_d:
                btaps.append(E.dma("gpsimd", tap_d["t"][_ts(j), :], TB[b][:],
                                   deps=[last_t], slot="tap"))

        E.barrier()
        E.flush(block)
        p5.close()
        if stop_after == "B":
            return nc

        p6 = ExitStack()
        P = p6.enter_context
        p_sc2 = P(nc.psum_tensor("p_sc2", [128, 128], F32))
        ev = E.op("tensor", lambda t, tp=tpar: t.transpose(
            p_sc2[0:1, 0:128], AMT[:, tp:tp + 1], IDF[:]), deps=[prev_amt], inc=1)
        e_row2 = E.op("vector", lambda v: v.tensor_copy(CCROW[0:1, :],
                                                        p_sc2[0:1, 0:128]),
                      deps=[ev], inc=1)
        e_w2 = E.dma("gpsimd", cc2i.ap(), CCROW[0:1, :], deps=[e_row2], slot="ccd")
        e_cc2 = E.op("gpsimd", lambda gg: gg.collective_compute(
            "AllReduce", ALU.max, replica_groups=[list(range(NCORE))],
            ins=[cc2i.ap()], outs=[cc2o.ap()]), deps=[e_w2], sem="s_cc", inc=1,
            async_ev=True)
        e_r2 = E.dma("gpsimd", CCROW[0:1, :], cc2o.ap(), deps=[e_cc2], slot="ccd")
        e_gm2 = E.op("vector", lambda v: v.reduce_max(SCAL[0:1, 4:5], CCROW[0:1, :],
                                                      axis=AX), deps=[e_r2], inc=1)
        e_sd = E.op("vector", lambda v: v.tensor_mul(SCAL[0:1, 5:6], SCAL[0:1, 4:5],
                                                     SCAL[0:1, 11:12]),
                    deps=[e_gm2, e_sgd], inc=1)
        e_sd2 = E.op("vector", lambda v: v.tensor_scalar(
            SCAL[0:1, 5:6], SCAL[0:1, 5:6], 1.0 / 127.0, None, ALU.mult),
            deps=[e_sd], inc=1)
        e_isd = E.op("vector", lambda v: v.reciprocal(SCAL[0:1, 6:7], SCAL[0:1, 5:6]),
                     deps=[e_sd2], inc=1)
        bc2 = []
        prevcp2 = e_isd
        for (col, dst) in ((5, SDCOL), (6, INVSD)):
            mm = E.op("tensor", lambda t, col=col: t.matmul(
                p_sc2[:, 0:1], ONESF[0:1, :], SCAL[0:1, col:col + 1],
                start=True, stop=True), deps=[e_isd, prevcp2], inc=1)
            prevcp2 = E.op("vector", lambda v, dst=dst: v.tensor_copy(
                dst[:], p_sc2[:, 0:1]), deps=[mm], inc=1)
            bc2.append(prevcp2)
        if "scal" in tap_d:
            e_dbg1 = E.op("vector", lambda v: v.tensor_copy(SCAL[0:1, 12:13], SXCOL[0:1, 0:1]),
                          deps=bc1, inc=1)
            e_dbg2 = E.op("vector", lambda v: v.tensor_copy(SCAL[0:1, 13:14], INVSX[0:1, 0:1]),
                          deps=bc1, inc=1)
            E.dma("gpsimd", tap_d["scal"], SCAL[0:1, 0:16],
                  deps=[e_isd, e_dbg1, e_dbg2] + bc2, slot="tap")

        E.barrier()
        E.flush(block)
        p6.close()
        if stop_after == "CC2":
            return nc

        p7 = ExitStack()
        P = p7.enter_context
        WDT = P(nc.sbuf_tensor("WDT", [128, 22 * 1024], BF16))
        Q2S = P(nc.sbuf_tensor("Q2S", [128, IFF], F32))
        Q2I = P(nc.sbuf_tensor("Q2I", [128, IFF], I32))
        Q2B = P(nc.sbuf_tensor("Q2B", [128, IFF], BF16))
        Q2T = P(nc.sbuf_tensor("Q2T", [128, IFF], BF16))
        OB = [P(nc.sbuf_tensor(f"OB{i}", [128, D], F32)) for i in range(2)]
        p_q2 = P(nc.psum_tensor("p_q2", [128, 2048], BF16))
        p_dn = P(nc.psum_tensor("p_dn", [128, 1024], F32))

        wd_lds = []
        for c in range(22):
            wd_lds.append(E.dmat("sync", WDT[:, c * 1024:(c + 1) * 1024],
                                 d_qwd[0:D, c * 128:(c + 1) * 128],
                                 deps=[qspill["last_d"]] if c == 0 else (),
                                 slot="wdl"))
        ob_free = [None, None]
        prev_q2t = {0: None, 1: None}
        prev_q2b = None
        prev_dn = {0: None, 1: None}
        ctaps = []
        for j in range(NT):
            b = j % 2
            ldt = E.dma("sync", TB[b][:], d_t[_ts(j), :],
                        deps=[t_dmas[j]] + ([tb_free[b]] if tb_free[b] else []) +
                             ctaps,
                        slot=f"tl{b}")
            ctaps = []
            ldxm = E.dma("sync", XM[b][:], d_xmid[_ts(j), :],
                         deps=[xm_free[b]] if xm_free[b] else (), slot=f"xm{b}")
            e_z2 = E.op("vector", lambda v, b=b: v.tensor_scalar(
                Q2S[:], TB[b][:], INVSD[:], None, ALU.mult),
                deps=[ldt] + (bc2 if j == 0 else []), inc=1)
            tb_free[b] = e_z2
            e_q2i = E.op("vector", lambda v: v.tensor_scalar(
                Q2I[:], Q2S[:], 127.49, -128.49, ALU.min, ALU.max),
                deps=[e_z2] + ([prev_q2b] if prev_q2b else []), inc=1)
            e_q2b = E.op("scalar", lambda s: s.activation(Q2B[:], Q2I[:], AF.Copy),
                         deps=[e_q2i, prev_q2t[0], prev_q2t[1]], inc=1)
            prev_q2b = e_q2b
            if "q2" in tap_d:
                ctaps.append(E.dma("gpsimd", tap_d["q2"][_ts(j), :], Q2B[:],
                                   deps=[e_q2b], slot="tap"))
            bevs = []
            for bi, (c0, cnt) in enumerate(((0, 8), (8, 8), (16, 6))):
                par = bi % 2
                tev = None
                for c in range(cnt):
                    cc = c0 + c
                    tev = E.op("tensor", lambda t, cc=cc, c=c, par=par: t.transpose(
                        p_q2[:, par * 1024 + c * 128: par * 1024 + (c + 1) * 128],
                        Q2B[:, _ts(cc)], IDB[:]),
                        deps=[e_q2b] + ([prev_q2t[par]] if prev_q2t[par] else [])
                        if c == 0 else [],
                        inc=(1 if c == cnt - 1 else 0))
                bevs.append(E.op("vector", lambda v, c0=c0, cnt=cnt, par=par: v.tensor_copy(
                    Q2T[:, c0 * 128:(c0 + cnt) * 128],
                    p_q2[:, par * 1024:par * 1024 + cnt * 128]),
                    deps=[tev], inc=1))
                prev_q2t[par] = bevs[-1]
            for n in range(2):
                lastm = None
                for k in range(22):
                    lastm = E.op("tensor", lambda t, n=n, k=k: t.matmul(
                        p_dn[:, _ts(n, 512)], Q2T[:, _ts(k)],
                        WDT[:, k * 1024 + n * 512: k * 1024 + (n + 1) * 512],
                        start=(k == 0), stop=(k == 21)),
                        deps=(bevs + ([prev_dn[n]] if prev_dn[n] else []) +
                              ([wd_lds[-1]] if j == 0 else [])) if k == 0 else [],
                        inc=(1 if k == 21 else 0))
                prev_dn[n] = E.op("vector", lambda v, b=b, n=n: v.scalar_tensor_tensor(
                    OB[b][:, _ts(n, 512)], p_dn[:, _ts(n, 512)], SDCOL[:],
                    XM[b][:, _ts(n, 512)], ALU.mult, ALU.add),
                    deps=[lastm, ldxm] + bc2 +
                         ([ob_free[b]] if (n == 0 and ob_free[b]) else []), inc=1)
            xm_free[b] = prev_dn[1]
            ob_free[b] = E.dma("sync", d_y[_ts(j), :], OB[b][:],
                               deps=[prev_dn[0], prev_dn[1]], slot=f"o{b}")

        for sl in ("d_o0", "d_o1") + (("d_tap",) if tap_d else ()):
            if sl in E.sems:
                E.op("sync", lambda e, h=E.sems[sl], v=E.cnt[sl]: e.wait_ge(h, v))
        E.barrier()
        E.flush(block)
        p7.close()

    return nc


def host_constants():
    bf = ml_dtypes.bfloat16
    idb = np.eye(128, dtype=bf)
    idf = np.eye(128, dtype=np.float32)
    ltri = np.triu(np.ones((128, 128), dtype=np.float32)).astype(bf)
    onesb = np.ones((1, 128), dtype=bf)
    onesf = np.ones((1, 128), dtype=np.float32)
    cmw = np.zeros((1, TSEQ * 128), dtype=np.float32)
    for jj in range(TSEQ):
        cmw[0, jj * 128:(jj + 1) * 128] = 128.0 * jj
    cmw = cmw.astype(bf)
    pos = np.arange(1, TSEQ * 128 + 1, dtype=np.float64).reshape(TSEQ, 128)
    posc = (1.0 / np.sqrt(pos)).astype(np.float32)
    invp = (1.0 / pos).astype(np.float32)
    cons = np.zeros((128, 4), dtype=np.float32)
    cons[:, 0] = HALF_PI
    cons[:, 1] = -1.0
    cons[:, 2] = -0.5
    return dict(idb=idb, idf=idf, ltri=ltri, onesb=onesb, onesf=onesf,
                cmw=cmw, posc=posc, invp=invp, cons=cons)


def make_in_maps(inputs):
    c = host_constants()
    x = np.ascontiguousarray(np.asarray(inputs["L0_attn_x"], dtype=np.float32))
    f32 = np.float32
    base = {
        "W_omega": np.ascontiguousarray(inputs["L0_attn_W_omega"], dtype=f32),
        "b_omega": np.ascontiguousarray(inputs["L0_attn_b_omega"], dtype=f32).reshape(1, D),
        "log_scale": np.ascontiguousarray(inputs["L0_attn_log_scale"], dtype=f32).reshape(1, D),
        "ln_gamma": np.ascontiguousarray(inputs["L0_attn_ln_gamma"], dtype=f32).reshape(1, CTX),
        "ln_beta": np.ascontiguousarray(inputs["L0_attn_ln_beta"], dtype=f32).reshape(32, 128),
        "W1": np.ascontiguousarray(inputs["L0_attn_W1"], dtype=f32),
        "b1": np.ascontiguousarray(inputs["L0_attn_b1"], dtype=f32).reshape(1, H1),
        "W2": np.ascontiguousarray(inputs["L0_attn_W2"], dtype=f32),
        "b2": np.ascontiguousarray(inputs["L0_attn_b2"], dtype=f32).reshape(1, D),
        "clip_x": np.ascontiguousarray(inputs["L0_ffn_clip_x"], dtype=f32).reshape(1, 1),
        "clip_down": np.ascontiguousarray(inputs["L0_ffn_clip_down"], dtype=f32).reshape(1, 1),
        "w_gate": np.ascontiguousarray(inputs["L0_ffn_w_gate"], dtype=f32),
        "w_up": np.ascontiguousarray(inputs["L0_ffn_w_up"], dtype=f32),
        "w_down": np.ascontiguousarray(inputs["L0_ffn_w_down"], dtype=f32),
        **c,
    }
    in_maps = []
    for core in range(NCORE):
        m = dict(base)
        m["x"] = np.ascontiguousarray(x[2 * core:2 * core + 2].reshape(NT * 128, D))
        in_maps.append(m)
    return in_maps


def kernel(**inputs):
    in_maps = make_in_maps(inputs)
    nc = build_program(taps=tuple(DEBUG_TAPS))
    res = run_bass_kernel_spmd(nc, in_maps, core_ids=list(range(NCORE)))
    out = np.empty((16, 2048, D), dtype=np.float32)
    for core in range(NCORE):
        out[2 * core:2 * core + 2] = np.asarray(
            res.results[core]["y"], dtype=np.float32).reshape(2, 2048, D)
    kernel._last_results = res
    return out
